# version 01
# baseline (speedup 1.0000x reference)
import itertools
import math
import numpy as np
from contextlib import ExitStack

import ml_dtypes
import concourse.bass as bass
import concourse.mybir as mybir
from concourse.bass_utils import run_bass_kernel_spmd

F32 = mybir.dt.float32
BF16 = mybir.dt.bfloat16
I32 = mybir.dt.int32
AF = mybir.ActivationFunctionType
ALU = mybir.AluOpType
AX = mybir.AxisListType
BF = ml_dtypes.bfloat16

NCORES = 8
B, T, D = 4, 4096, 1024
TH = T // 2


def ceil_div(a, b):
    return (a + b - 1) // b



class Arena:
    def __init__(self, tensor, words):
        self.t = tensor
        self.words = words
        self.off = 0
        self.peak = 0
        self.marks = []

    def alloc(self, n, parts=128):
        n_al = ceil_div(n, 16) * 16
        assert self.off + n_al <= self.words, (
            f"arena {self.t.name} overflow: {self.off}+{n_al} > {self.words}")
        ap = self.t.ap()[0:parts, self.off:self.off + n]
        self.off += n_al
        self.peak = max(self.peak, self.off)
        return ap

    def mark(self):
        self.marks.append(self.off)

    def reset(self):
        self.off = self.marks.pop()


class Prog:

    ENGINES = ("sync", "gpsimd", "vector", "scalar", "tensor")

    def __init__(self, nc, f32_words=20000, bf16_words=60000, i32_words=0):
        self.nc = nc
        self.stack = ExitStack()
        self._ctr = itertools.count()
        self.ops = {e: [] for e in self.ENGINES}
        self.sems = {}
        self.waited = {e: {} for e in self.ENGINES}
        for s in ("v", "s", "t", "g", "q", "dS", "dG"):
            self._mksem(s)
        self.dma_sems = ["dS", "dG"]
        self.f32 = Arena(self._sbuf("arena_f32", f32_words, F32), f32_words)
        self.bf16 = Arena(self._sbuf("arena_bf16", bf16_words, BF16), bf16_words)
        self.i32 = Arena(self._sbuf("arena_i32", max(i32_words, 16), I32),
                         max(i32_words, 16)) if i32_words >= 0 else None
        self.psumT = self.stack.enter_context(
            self.nc.psum_tensor("psumT", [128, 4096], F32))
        self.psum = [self.psumT.ap()[:, 512 * i:512 * (i + 1)] for i in range(8)]

    def _sbuf(self, name, words, dtype):
        return self.stack.enter_context(self.nc.sbuf_tensor(name, [128, words], dtype))

    def _mksem(self, name):
        h = self.stack.enter_context(self.nc.semaphore(name=f"sem_{name}"))
        self.sems[name] = [h, 0]

    AUTO_SER = {"vector": "v", "scalar": "s", "gpsimd": "g"}

    def emit(self, engine, fn, waits=(), inc=None, inck=1, kind="compute"):
        waits = [w for w in waits if w is not None]
        if fn is None:
            self.ops[engine].append((None, tuple(waits), None, 0, None))
            return None
        autosem = self.AUTO_SER.get(engine) if kind == "compute" else None
        selfwait = None
        if autosem is not None:
            assert inc is None or inc == autosem
            selfwait = (autosem, self.sems[autosem][1])
            inc, inck = autosem, 1
        if inc is not None:
            self.sems[inc][1] += inck
        n_after = self.sems[inc][1] if inc is not None else None
        self.ops[engine].append((fn, tuple(waits), inc, inck, selfwait))
        if inc is not None:
            return (inc, n_after)
        return None

    def now(self, semname):
        return self.sems[semname][1]

    def dsem(self, name):
        if name not in self.sems:
            self._mksem(name)
            self.dma_sems.append(name)
        return name

    def dma(self, queue, out_ap, in_ap, waits=(), sem=None):
        semname = sem if sem is not None else ("dS" if queue == "sync" else "dG")
        if sem is not None:
            self.dsem(sem)
        def fn(eng, out_ap=out_ap, in_ap=in_ap):
            return eng.dma_start(out=out_ap, in_=in_ap)
        return self.emit(queue, fn, waits=waits, inc=semname, inck=16, kind="dma")

    def drain_dmas(self, engine="sync"):
        self.emit(engine, None, waits=[(s, self.now(s)) for s in self.dma_sems])

    def barrier(self, tag=""):
        marks = {}
        for e in ("gpsimd", "vector", "scalar"):
            marks[e] = self.emit(e, lambda eng: eng.nop())
        marks["tensor"] = ("t", self.now("t"))
        marks["sync"] = self.emit("sync", lambda eng: eng.nop(), inc="q", kind="dma")
        dmaw = [(s, self.now(s)) for s in self.dma_sems]
        for e in self.ENGINES:
            self.emit(e, None, waits=[marks[e2] for e2 in self.ENGINES if e2 != e] + dmaw)

    def build(self):
        nc = self.nc
        with nc.Block() as block:
            for engine in self.ENGINES:
                def mk(engine):
                    def body(eng):
                        for fn, waits, inc, inck, selfwait in self.ops[engine]:
                            todo = []
                            for semname, val in waits:
                                prev = self.waited[engine].get(semname, -1)
                                if val > prev:
                                    todo.append((semname, val))
                                    self.waited[engine][semname] = val
                            if fn is None:
                                for semname, val in todo:
                                    eng.wait_ge(self.sems[semname][0], val)
                                continue
                            attach = None
                            if selfwait is not None:
                                attach = selfwait
                            elif todo:
                                attach = todo.pop()
                            for semname, val in todo:
                                eng.wait_ge(self.sems[semname][0], val)
                            ins = fn(eng)
                            if attach is not None and attach[1] > 0:
                                ins._wait_ge(self.sems[attach[0]][0], attach[1])
                            if inc is not None:
                                ins.then_inc(self.sems[inc][0], inck)
                    return body
                getattr(block, engine)(mk(engine))
        self.stack.close()



_WSEM = itertools.count()


def load_weight_slabs(p, hbm_ap, K, M, dtype=BF16, queue="gpsimd", gate=None):
    nk = ceil_div(K, 128)
    slabs = []
    wait = None
    semname = f"wl{next(_WSEM)}"
    for k in range(nk):
        k0, k1 = k * 128, min(k * 128 + 128, K)
        buf = p.bf16.alloc(M) if dtype == BF16 else p.f32.alloc(M)
        wait = p.dma(queue, buf[0:k1 - k0, :], hbm_ap[k0:k1, :], sem=semname,
                     waits=[gate] if (gate and k == 0) else [])
        slabs.append(buf[0:k1 - k0, :])
    return slabs, wait


def gemm_fm(p, slabs, w_wait, rhs, Mtiles, out_cb, Nt=512, mslice=None,
            extra_acc=None, rhs_wait=None, psum_sel=None):
    Tfree = rhs[0].shape[1]
    ntt = ceil_div(Tfree, Nt)
    nk = len(slabs)
    psum_busy = {}
    for m in range(Mtiles):
        c0, c1 = (m * 128, (m + 1) * 128) if mslice is None else mslice(m)
        for tt in range(ntt):
            n0, n1 = tt * Nt, min(tt * Nt + Nt, Tfree)
            psi = psum_sel((m * ntt + tt)) if psum_sel else ((m * ntt + tt) % 4)
            psum = p.psum[psi]
            mm_mark = None
            for k in range(nk):
                waits = []
                if k == 0:
                    waits = [w for w in (w_wait, rhs_wait) if w is not None]
                    if psi in psum_busy:
                        waits.append(psum_busy[psi])
                last = (k == nk - 1)
                def fn(eng, k=k, c0=c0, c1=c1, n0=n0, n1=n1, psum=psum,
                       first=(k == 0), last=(last and extra_acc is None)):
                    return eng.matmul(psum[0:c1 - c0, 0:n1 - n0],
                                      slabs[k][:, c0:c1], rhs[k][:, n0:n1],
                                      start=first, stop=last)
                mm_mark = p.emit("tensor", fn, waits=waits,
                                 inc="t" if (last and extra_acc is None) else None)
            if extra_acc is not None:
                mm_mark = extra_acc(m, tt, psum, n0, n1)
            done = out_cb(m, tt, psum[0:c1 - c0, 0:n1 - n0], n0, n1, mm_mark)
            assert done is not None, "out_cb must return the mark of its last psum read"
            psum_busy[psi] = done


def bcast_row(p, row_ap, ones1, ncols, out_psum, row_wait=None, acc=False, scale=None):
    def fn(eng):
        return eng.matmul(out_psum[:, 0:row_ap.shape[1]], ones1[0:1, :], row_ap,
                          start=not acc, stop=True)
    waits = [row_wait] if row_wait else []
    return p.emit("tensor", fn, waits=waits, inc="t")


def gemm_cols(p, hbm_w, K, rhs, rhs_wait, mt_list, out_cb, Tlo=0, Thi=None,
              group_mt=8, Nt=512, extra_acc=None, ctx=None, mcols=None):
    nk = ceil_div(K, 128)
    Thi = rhs[0].shape[1] if Thi is None else Thi
    ntt = ceil_div(Thi - Tlo, Nt)
    groups = [mt_list[i:i + group_mt] for i in range(0, len(mt_list), group_mt)]
    if ctx is None:
        ctx = {}
    if "gbuf" not in ctx:
        ctx["gbuf"] = [[p.bf16.alloc(group_mt * 128) for _ in range(nk)] for _ in range(2)]
        ctx["gsem"] = [f"gc{next(_WSEM)}_0", f"gc{next(_WSEM)}_1"]
        ctx["gfree"] = [None, None]
        ctx["psum_busy"] = {}
    gbuf = ctx["gbuf"]
    gsem = ctx["gsem"]
    gfree = ctx["gfree"]
    psum_busy = ctx["psum_busy"]
    gwait = [None, None]

    def wcols(m):
        return (m * 128, (m + 1) * 128) if mcols is None else mcols[m]

    def load_group(gi, mts):
        bi = gi % 2
        w = None
        for k in range(nk):
            k0, k1 = k * 128, min(k * 128 + 128, K)
            for j, m in enumerate(mts):
                c0, c1 = wcols(m)
                waits = [gfree[bi]] if (k == 0 and j == 0 and gfree[bi]) else []
                w = p.dma("gpsimd", gbuf[bi][k][0:k1 - k0, j * 128:j * 128 + (c1 - c0)],
                          hbm_w[k0:k1, c0:c1], waits=waits, sem=gsem[bi])
        gwait[bi] = w

    load_group(0, groups[0])
    for gi, mts in enumerate(groups):
        bi = gi % 2
        if gi + 1 < len(groups):
            load_group(gi + 1, groups[gi + 1])
        mm = None
        for j, m in enumerate(mts):
            for tt in range(ntt):
                n0 = Tlo + tt * Nt
                n1 = min(n0 + Nt, Thi)
                psi = (m * ntt + tt) % 4
                psum = p.psum[psi]
                c0, c1 = wcols(m)
                for k in range(nk):
                    waits = []
                    if k == 0:
                        waits = [w for w in (gwait[bi], rhs_wait) if w is not None]
                        if psi in psum_busy:
                            waits.append(psum_busy[psi])
                    last = (k == nk - 1)
                    def fn(eng, k=k, j=j, n0=n0, n1=n1, psum=psum, bi=bi, c0=c0, c1=c1,
                           first=(k == 0), last=(last and extra_acc is None)):
                        return eng.matmul(psum[0:c1 - c0, 0:n1 - n0],
                                          gbuf[bi][k][:, j * 128:j * 128 + (c1 - c0)],
                                          rhs[k][:, n0:n1], start=first, stop=last)
                    mm = p.emit("tensor", fn, waits=waits,
                                inc="t" if (last and extra_acc is None) else None)
                if extra_acc is not None:
                    mm = extra_acc(m, tt, psum, n0, n1)
                done = out_cb(m, tt, psum, n0, n1, mm)
                assert done is not None
                psum_busy[psi] = done
        gfree[bi] = mm



def ln_stats_stream(p, x_hbm, xrot, ones_a, ones_b, TF, cst_mark, nchunk=8,
                    banks=(3, 4, 5, 6, 7), eps=1e-5, xsem=("xl0", "xl1"), gates=None):
    ntt = ceil_div(TF, 512)
    assert ntt <= len(banks)
    xsq = p.f32.alloc(TF)
    st2 = p.f32.alloc(TF, parts=2)
    s2c = p.f32.alloc(TF, parts=1)
    rstd = p.f32.alloc(TF, parts=1)
    mu = st2[0:1, :]
    cons = {}
    for c in range(nchunk):
        buf = xrot[c % 2]
        waits = [cons[c - 2]] if c - 2 in cons else ([gates[c]] if gates else [])
        w = p.dma("sync", buf, x_hbm[c * 128:(c + 1) * 128, :], waits=waits,
                  sem=xsem[c % 2])
        def fsq(eng, buf=buf):
            return eng.activation(xsq, buf, AF.Square)
        sq = p.emit("scalar", fsq, waits=[w], inc="s")
        tm = None
        for tt in range(ntt):
            n0, n1 = tt * 512, min(tt * 512 + 512, TF)
            bank = p.psum[banks[tt]]
            def f1(eng, buf=buf, n0=n0, n1=n1, bank=bank, first=(c == 0)):
                return eng.matmul(bank[0:2, 0:n1 - n0], ones_a, buf[:, n0:n1],
                                  start=first, stop=False)
            p.emit("tensor", f1, waits=[w, cst_mark])
            def f2(eng, n0=n0, n1=n1, bank=bank, last=(c == nchunk - 1)):
                return eng.matmul(bank[0:2, 0:n1 - n0], ones_b, xsq[:, n0:n1],
                                  start=False, stop=last)
            tm = p.emit("tensor", f2, waits=[sq], inc="t" if tt == ntt - 1 else None)
        cons[c] = tm
        p.emit("scalar", None, waits=[tm])
    tmark = ("t", p.now("t"))
    for tt in range(ntt):
        n0, n1 = tt * 512, min(tt * 512 + 512, TF)
        bank = p.psum[banks[tt]]
        def fc(eng, n0=n0, n1=n1, bank=bank):
            return eng.tensor_copy(st2[:, n0:n1], bank[0:2, 0:n1 - n0])
        p.emit("vector", fc, waits=[tmark])
    inv_n = 1.0 / (nchunk * 128.0)
    p.emit("vector", lambda eng: eng.tensor_scalar_mul(st2[0:2, :], st2[0:2, :], inv_n))
    vm = p.emit("vector", lambda eng: eng.nop(), inc="v")
    dw = p.dma("sync", s2c, st2[1:2, :], waits=[vm], sem="s2c")
    sm = p.emit("scalar", lambda eng: eng.activation(rstd, mu, AF.Square), waits=[vm])
    p.emit("vector", lambda eng: eng.tensor_sub(s2c, s2c, rstd), waits=[sm, dw])
    p.emit("vector", lambda eng: eng.tensor_scalar_add(s2c, s2c, eps))
    vm2 = p.emit("vector", lambda eng: eng.nop(), inc="v")
    sm2 = p.emit("scalar", lambda eng: eng.activation(xsq[0:1, :], s2c, AF.Sqrt),
                 waits=[vm2])
    p.emit("vector", lambda eng: eng.reciprocal(rstd, xsq[0:1, :]), waits=[sm2])
    ready = p.emit("vector", lambda eng: eng.nop(), inc="v")
    return mu, rstd, ready, cons


def emit_ln_apply(p, x_hbm, xrot, mu, rstd, ready, ones1, cst_mark, xn_out,
                  lnp_sb, lnp_wait, TF, nchunk=8, bank0=0, bank1=1,
                  xsem=("xl0", "xl1"), gates=None):
    ntt = ceil_div(TF, 512)
    cons = {}
    bc_busy = {}
    for c in range(nchunk):
        buf = xrot[c % 2]
        waits = [cons[c - 2]] if c - 2 in cons else ([gates[c]] if gates else [])
        w = p.dma("sync", buf, x_hbm[c * 128:(c + 1) * 128, :], waits=waits,
                  sem=xsem[c % 2])
        for tt in range(ntt):
            n0, n1 = tt * 512, min(tt * 512 + 512, TF)
            pm = p.psum[bank0]
            pr = p.psum[bank1]
            bw = [ready, cst_mark] + ([bc_busy[0]] if 0 in bc_busy else [])
            def b1(eng, n0=n0, n1=n1, pm=pm):
                return eng.matmul(pm[:, 0:n1 - n0], ones1, mu[0:1, n0:n1],
                                  start=True, stop=True)
            p.emit("tensor", b1, waits=bw)
            def b2(eng, n0=n0, n1=n1, pr=pr):
                return eng.matmul(pr[:, 0:n1 - n0], ones1, rstd[0:1, n0:n1],
                                  start=True, stop=True)
            tb = p.emit("tensor", b2, inc="t")
            def v1(eng, buf=buf, n0=n0, n1=n1, pm=pm):
                return eng.tensor_sub(buf[:, n0:n1], buf[:, n0:n1], pm[:, 0:n1 - n0])
            p.emit("vector", v1, waits=[w, tb])
            def v2(eng, c=c, buf=buf, n0=n0, n1=n1, pr=pr):
                return eng.tensor_mul(xn_out[c][:, n0:n1], buf[:, n0:n1], pr[:, 0:n1 - n0])
            p.emit("vector", v2)
            bc_busy[0] = p.emit("vector", lambda eng: eng.nop(), inc="v")
        if lnp_sb is not None:
            def v3(eng, c=c):
                return eng.tensor_scalar(xn_out[c], xn_out[c], lnp_sb[:, c:c + 1],
                                         lnp_sb[:, 8 + c:9 + c], ALU.mult, ALU.add)
            p.emit("vector", v3, waits=[lnp_wait])
        cons[c] = p.emit("vector", lambda eng: eng.nop(), inc="v")
    return ("v", p.now("v"))


def build_L0a(nc, stop_after=99):
    p = Prog(nc, f32_words=14000, bf16_words=76000)
    TF = TH + 3
    x_in = nc.dram_tensor("x_in", [D, TF], F32, kind="ExternalInput").ap()
    w_inproj = nc.dram_tensor("w_inproj", [D, 4096], F32, kind="ExternalInput").ap()
    lnp = nc.dram_tensor("lnp", [128, 16], F32, kind="ExternalInput").ap()
    convw = nc.dram_tensor("convw", [128, 64], F32, kind="ExternalInput").ap()
    convb = nc.dram_tensor("convb", [128, 16], F32, kind="ExternalInput").ap()
    w_xproj = nc.dram_tensor("w_xproj", [2048, 96], F32, kind="ExternalInput").ap()
    w_dtproj = nc.dram_tensor("w_dtproj", [64, 2048], F32, kind="ExternalInput").ap()
    dtb = nc.dram_tensor("dtb", [128, 16], F32, kind="ExternalInput").ap()
    o_z = nc.dram_tensor("o_z", [2048, TH], BF16, kind="ExternalOutput").ap()
    o_xc = nc.dram_tensor("o_xc", [2048, TH], BF16, kind="ExternalOutput").ap()
    o_dt = nc.dram_tensor("o_dt", [2048, TH], BF16, kind="ExternalOutput").ap()
    o_w = nc.dram_tensor("o_w", [2048, TH], BF16, kind="ExternalOutput").ap()
    o_bc = nc.dram_tensor("o_bc", [32, TH], BF16, kind="ExternalOutput").ap()

    ones_a = p.f32.alloc(2)
    ones_b = p.f32.alloc(2)
    p.emit("vector", lambda eng: eng.memset(ones_a, 0.0))
    p.emit("vector", lambda eng: eng.memset(ones_a[:, 0:1], 1.0))
    p.emit("vector", lambda eng: eng.memset(ones_b, 0.0))
    p.emit("vector", lambda eng: eng.memset(ones_b[:, 1:2], 1.0))
    ones1 = p.f32.alloc(128, parts=1)
    p.emit("vector", lambda eng: eng.memset(ones1, 1.0))
    cst_mark = p.emit("vector", lambda eng: eng.nop(), inc="v")

    lnp_sb = p.f32.alloc(16)
    p.dma("sync", lnp_sb, lnp)
    convw_sb = p.f32.alloc(64)
    p.dma("sync", convw_sb, convw)
    convb_sb = p.f32.alloc(16)
    p.dma("sync", convb_sb, convb)
    dtb_sb = p.f32.alloc(16)
    p.dma("sync", dtb_sb, dtb)
    param_wait = ("dS", p.now("dS"))
    lnp_wait = convw_wait = convb_wait = dtb_wait = param_wait

    p.f32.mark()
    xrot = [p.f32.alloc(TF) for _ in range(2)]
    mu, rstd, stats_ready, xcons = ln_stats_stream(p, x_in, xrot, ones_a, ones_b, TF, cst_mark)
    if stop_after <= 1:
        p.drain_dmas()
        p.build(); return p

    xn = [p.bf16.alloc(TF) for _ in range(8)]
    xn_wait = emit_ln_apply(p, x_in, xrot, mu, rstd, stats_ready, ones1, cst_mark,
                            xn, lnp_sb, lnp_wait, TF,
                            gates={0: xcons[6], 1: xcons[7]})
    if stop_after <= 2:
        p.drain_dmas()
        p.build(); return p
    p.barrier()
    p.f32.reset()

    xc = [p.bf16.alloc(TH) for _ in range(16)]
    p.bf16.mark()
    xirot = [p.bf16.alloc(TF) for _ in range(3)]
    zrot = [p.bf16.alloc(TF) for _ in range(3)]
    tmpc = p.f32.alloc(TH)
    zdma = {}
    conv_done = {}
    xc_marks = []

    def out_cb(m, tt, psum, n0, n1, wait):
        if m < 16:
            xb = xirot[m % 3]
            waits = [wait]
            if m - 3 in conv_done:
                waits.append(conv_done[m - 3])
            def fe(eng, xb=xb, n0=n0, n1=n1, psum=psum):
                return eng.tensor_copy(xb[:, n0:n1], psum[:, 0:n1 - n0])
            mark = p.emit("vector", fe, waits=waits, inc="v")
            if n1 == TF:
                cw = [convw_wait] if m == 0 else []
                def f0(eng, m=m, xb=xb):
                    return eng.tensor_scalar_mul(tmpc, xb[:, 0:TH], convw_sb[:, 4 * m:4 * m + 1])
                p.emit("vector", f0, waits=cw + ([conv_done[m - 1]] if (m - 1) in conv_done else []))
                for j in range(1, 4):
                    def fj(eng, m=m, j=j, xb=xb):
                        return eng.scalar_tensor_tensor(tmpc, xb[:, j:j + TH],
                                                        convw_sb[:, 4 * m + j:4 * m + j + 1],
                                                        tmpc, ALU.mult, ALU.add)
                    p.emit("vector", fj)
                vmark = p.emit("vector", lambda eng: eng.nop(), inc="v")
                def fs(eng, m=m):
                    return eng.activation(xc[m], tmpc, AF.Silu, bias=convb_sb[:, m:m + 1])
                smark = p.emit("scalar", fs, waits=[vmark, convb_wait], inc="s")
                xc_marks.append(smark)
                p.dma("sync", o_xc[m * 128:(m + 1) * 128, :], xc[m], waits=[smark])

                conv_done[m] = smark
                mark = smark
            return mark
        else:
            zb = zrot[(m - 16) % 3]
            waits = [wait]
            if (m - 16 - 3) in zdma:
                waits.append(zdma[m - 16 - 3])
            def fe(eng, zb=zb, n0=n0, n1=n1, psum=psum):
                return eng.tensor_copy(zb[:, n0:n1], psum[:, 0:n1 - n0])
            mark = p.emit("vector", fe, waits=waits, inc="v")
            if n1 == TF:
                zdma[m - 16] = p.dma("sync", o_z[(m - 16) * 128:(m - 15) * 128, :],
                                     zb[:, 3:TF], waits=[mark], sem=f"zo{(m - 16) % 3}")
            return mark

    gemm_cols(p, w_inproj, D, xn, xn_wait, list(range(32)), out_cb, group_mt=4)
    xc_wait = ("s", p.now("s"))
    if stop_after <= 3:
        p.drain_dmas()
        p.build(); return p
    p.barrier()
    p.bf16.reset()

    xpro, xpro_wait = load_weight_slabs(p, w_xproj, 2048, 96)
    dbc = p.f32.alloc(TH, parts=96)
    dbc_bf = p.bf16.alloc(TH, parts=64)
    eb = {}
    for tt in range(4):
        n0, n1 = tt * 512, tt * 512 + 512
        psum = p.psum[tt % 2]
        for k in range(16):
            def fm(eng, k=k, n0=n0, n1=n1, psum=psum, first=(k == 0), last=(k == 15)):
                return eng.matmul(psum[0:96, 0:512], xpro[k], xc[k][:, n0:n1],
                                  start=first, stop=last)
            waits = ([xpro_wait, xc_wait] + ([eb[tt % 2]] if tt % 2 in eb else [])) if k == 0 else []
            tmk = p.emit("tensor", fm, waits=waits, inc="t" if k == 15 else None)
        def fe(eng, n0=n0, n1=n1, psum=psum):
            return eng.tensor_copy(dbc[:, n0:n1], psum[0:96, 0:512])
        eb[tt % 2] = p.emit("vector", fe, waits=[tmk], inc="v")
    def fcast(eng):
        return eng.tensor_copy(dbc_bf, dbc[0:64, :])
    dbc_mark = p.emit("vector", fcast, inc="v")
    bc_bf = p.bf16.alloc(TH, parts=32)
    p.emit("vector", lambda eng: eng.tensor_copy(bc_bf, dbc[64:96, :]), waits=[], inc="v")
    bc_mark = ("v", p.now("v"))
    p.dma("sync", o_bc, bc_bf, waits=[bc_mark])
    if stop_after <= 4:
        p.drain_dmas()
        p.build(); return p
    p.barrier()

    dtp, dtp_wait = load_weight_slabs(p, w_dtproj, 64, 2048)
    p.f32.mark()
    t0 = p.f32.alloc(512)
    ab = p.f32.alloc(512)
    ex = p.f32.alloc(512)
    rl = p.f32.alloc(512)
    dtrot = [p.bf16.alloc(512) for _ in range(2)]
    wrot = [p.bf16.alloc(512) for _ in range(2)]
    dma_marks = {}
    for m in range(16):
        for tt in range(4):
            n0, n1 = tt * 512, tt * 512 + 512
            psum = p.psum[(m * 4 + tt) % 2 + 2]
            def fm(eng, m=m, n0=n0, n1=n1, psum=psum):
                return eng.matmul(psum[:, 0:512], dtp[0][:, m * 128:(m + 1) * 128],
                                  dbc_bf[:, n0:n1], start=True, stop=True)
            waits = [dtp_wait, dbc_mark]
            if ("ps", (m * 4 + tt) % 2 + 2) in dma_marks:
                waits.append(dma_marks[("ps", (m * 4 + tt) % 2 + 2)])
            tmk = p.emit("tensor", fm, waits=waits, inc="t")
            def f1(eng, m=m, psum=psum):
                return eng.activation(t0, psum[:, 0:512], AF.Identity, bias=dtb_sb[:, m:m + 1])
            f1_waits = [tmk, dtb_wait]
            if ("t0v", 0) in dma_marks:
                f1_waits.append(dma_marks[("t0v", 0)])
            sm1 = p.emit("scalar", f1, waits=f1_waits, inc="s")
            dma_marks[("ps", (m * 4 + tt) % 2 + 2)] = sm1
            p.emit("scalar", lambda eng: eng.activation(ab, t0, AF.Abs))
            sm2 = p.emit("scalar", lambda eng: eng.activation(ex, ab, AF.Exp, scale=-1.0), inc="s")
            sm3 = p.emit("scalar", lambda eng: eng.activation(ex, ex, AF.Ln, bias=1.0), waits=[sm2], inc="s")
            dtb_buf = dtrot[(m * 4 + tt) % 2]
            wb_buf = wrot[(m * 4 + tt) % 2]
            waits2 = [sm3]
            key = ("rot", (m * 4 + tt) % 2)
            if key in dma_marks:
                waits2.append(dma_marks[key])
            p.emit("vector", lambda eng: eng.tensor_single_scalar(rl, t0, 0.0, ALU.max), waits=waits2)
            def f4(eng, dtb_buf=dtb_buf):
                return eng.tensor_add(dtb_buf, rl, ex)
            p.emit("vector", f4)
            def f5(eng, dtb_buf=dtb_buf, wb_buf=wb_buf, m=m, n0=n0, n1=n1):
                return eng.tensor_mul(wb_buf, dtb_buf, xc[m][:, n0:n1])
            vm2 = p.emit("vector", f5, inc="v")
            dsem = f"dw{(m * 4 + tt) % 2}"
            d1 = p.dma("sync", o_dt[m * 128:(m + 1) * 128, n0:n1], dtb_buf, waits=[vm2], sem=dsem)
            d2 = p.dma("sync", o_w[m * 128:(m + 1) * 128, n0:n1], wb_buf, waits=[vm2], sem=dsem)
            dma_marks[key] = d2
            dma_marks[("t0v", 0)] = vm2
    p.drain_dmas()
    p.build()
    return p



def pack_pp(vec, nchunk):
    return np.ascontiguousarray(np.asarray(vec, np.float32).reshape(nchunk, 128).T)


def l0a_inputs(inp):
    x = np.asarray(inp["L0_attn_x"], np.float32)
    convw = np.asarray(inp["L0_attn_conv_w"], np.float32)
    cw = np.ascontiguousarray(
        convw.reshape(16, 128, 4).transpose(1, 0, 2).reshape(128, 64))
    common = {
        "w_inproj": np.ascontiguousarray(inp["L0_attn_in_proj_w"], np.float32),
        "lnp": np.concatenate([pack_pp(inp["L0_attn_ln_w"], 8),
                               pack_pp(inp["L0_attn_ln_b"], 8)], axis=1),
        "convw": cw,
        "convb": pack_pp(inp["L0_attn_conv_b"], 16),
        "w_xproj": np.ascontiguousarray(inp["L0_attn_x_proj_w"], np.float32),
        "w_dtproj": np.ascontiguousarray(inp["L0_attn_dt_proj_w"], np.float32),
        "dtb": pack_pp(inp["L0_attn_dt_proj_b"], 16),
    }
    maps = []
    for b in range(B):
        xt = x[b].T
        for h in range(2):
            t0 = h * TH
            if h == 0:
                xh = np.concatenate([np.zeros((D, 3), np.float32), xt[:, 0:TH]], axis=1)
            else:
                xh = xt[:, t0 - 3:t0 + TH]
            m = dict(common)
            m["x_in"] = np.ascontiguousarray(xh)
            maps.append(m)
    return maps


def build_L0scan(nc):
    p = Prog(nc, f32_words=16000, bf16_words=74000)
    DH = 1024
    i_dt = nc.dram_tensor("i_dt", [DH, T], BF16, kind="ExternalInput").ap()
    i_w = nc.dram_tensor("i_w", [DH, T], BF16, kind="ExternalInput").ap()
    i_xc = nc.dram_tensor("i_xc", [DH, T], BF16, kind="ExternalInput").ap()
    i_z = nc.dram_tensor("i_z", [DH, T], BF16, kind="ExternalInput").ap()
    i_bc = nc.dram_tensor("i_bc", [32, T], BF16, kind="ExternalInput").ap()
    i_alog = nc.dram_tensor("i_alog", [128, 128], F32, kind="ExternalInput").ap()
    i_dsk = nc.dram_tensor("i_dsk", [128, 8], F32, kind="ExternalInput").ap()
    i_patrep = nc.dram_tensor("i_patrep", [128, 16 * 128], BF16, kind="ExternalInput").ap()
    i_patred = nc.dram_tensor("i_patred", [128, 16 * 128], BF16, kind="ExternalInput").ap()
    i_rs = nc.dram_tensor("i_rs", [16, 2 * 128], BF16, kind="ExternalInput").ap()
    o_y = nc.dram_tensor("o_y", [DH, T], BF16, kind="ExternalOutput").ap()

    TS = 1024
    NU = T // TS

    alog = p.f32.alloc(128)
    dsk = p.f32.alloc(8)
    patrep = p.bf16.alloc(16 * 128)
    patred = p.bf16.alloc(16 * 128)
    rs = p.bf16.alloc(2 * 128, parts=16)
    p.dma("sync", alog, i_alog)
    p.dma("sync", dsk, i_dsk)
    p.dma("sync", patrep, i_patrep)
    p.dma("sync", patred, i_patred)
    p.dma("sync", rs, i_rs)
    bcB = p.bf16.alloc(T, parts=16)
    bcC = p.bf16.alloc(T, parts=16)
    p.dma("sync", bcB, i_bc[0:16, :])
    p.dma("sync", bcC, i_bc[16:32, :])
    cst_wait = ("dS", p.now("dS"))
    bc_w = cst_wait
    A = p.f32.alloc(128)
    sA = p.emit("scalar", lambda eng: eng.activation(A, alog, AF.Exp), waits=[cst_wait])
    vA = p.emit("vector", lambda eng: eng.tensor_scalar_mul(A, A, -1.0), waits=[sA])

    brep = p.bf16.alloc(T)
    crep = p.bf16.alloc(T)
    for sec in range(T // 512):
        n0 = sec * 512
        def fb(eng, n0=n0):
            return eng.matmul(p.psum[6][:, 0:512], rs[:, 0:128], bcB[:, n0:n0 + 512],
                              start=True, stop=True)
        mb = p.emit("tensor", fb, waits=[bc_w, cst_wait], inc="t")
        eb = p.emit("vector", lambda eng, n0=n0: eng.tensor_copy(brep[:, n0:n0 + 512],
                                                                 p.psum[6][:, 0:512]),
                    waits=[mb])
        def fc(eng, n0=n0):
            return eng.matmul(p.psum[7][:, 0:512], rs[:, 0:128], bcC[:, n0:n0 + 512],
                              start=True, stop=True)
        mc = p.emit("tensor", fc, waits=[eb], inc="t")
        p.emit("vector", lambda eng, n0=n0: eng.tensor_copy(crep[:, n0:n0 + 512],
                                                            p.psum[7][:, 0:512]),
              waits=[mc])
    bcrep_wait = ("v", p.now("v"))
    p.barrier()

    rot = {nm: [p.bf16.alloc(T) for _ in range(2)] for nm in ("dt", "w", "xc", "z")}
    hbm = {"dt": i_dt, "w": i_w, "xc": i_xc, "z": i_z}
    state = p.f32.alloc(16)
    afac = p.f32.alloc(TS)
    d1 = p.bf16.alloc(TS)
    hsec = p.bf16.alloc(TS)
    hc = p.bf16.alloc(TS)
    zs = p.f32.alloc(TS)
    tmpy = p.f32.alloc(TS)
    yout = [p.bf16.alloc(TS) for _ in range(2)]
    chunk_done = {}
    ydma = {}

    def load_chunk(c):
        waits = [chunk_done[c - 2]] if c - 2 in chunk_done else []
        ws = []
        for nm in ("dt", "w", "xc", "z"):
            ws.append(p.dma("gpsimd", rot[nm][c % 2], hbm[nm][c * 128:(c + 1) * 128, :],
                            waits=waits, sem=f"ch{c % 2}"))
            waits = []
        return ("ch" + str(c % 2), p.now("ch" + str(c % 2)))

    pre = load_chunk(0)
    chw = {0: pre}
    for c in range(8):
        if c + 1 < 8:
            chw[c + 1] = load_chunk(c + 1)
        cw = chw[c]
        dtc, wc, xcc, zc = (rot[nm][c % 2] for nm in ("dt", "w", "xc", "z"))
        zero_state = p.emit("vector", lambda eng: eng.memset(state, 0.0), waits=[])
        for u in range(NU):
            n0 = u * TS
            ps_y = p.psumT.ap()[:, 0:TS]
            ps_a = p.psumT.ap()[:, TS:2 * TS]
            ps_w = p.psumT.ap()[:, 2 * TS:3 * TS]
            red_marks = []
            prev_sa = prev_vd = prev_vs = None
            for t in range(16):
                for half in range(2):
                    h0 = half * 512
                    def fa(eng, t=t, n0=n0, h0=h0, dtc=dtc, ps_a=ps_a):
                        return eng.matmul(ps_a[:, h0:h0 + 512],
                                          patrep[:, t * 128:(t + 1) * 128],
                                          dtc[:, n0 + h0:n0 + h0 + 512],
                                          start=True, stop=True)
                    wa = [cw, cst_wait] if (t == 0 and half == 0) else []
                    if half == 0 and prev_sa is not None:
                        wa.append(prev_sa)
                    if half == 0 and ("sa", u) in ydma:
                        wa.append(ydma.pop(("sa", u)))
                    ma = p.emit("tensor", fa, waits=wa, inc="t" if half == 1 else None)
                    def fw(eng, t=t, n0=n0, h0=h0, wc=wc, ps_w=ps_w):
                        return eng.matmul(ps_w[:, h0:h0 + 512],
                                          patrep[:, t * 128:(t + 1) * 128],
                                          wc[:, n0 + h0:n0 + h0 + 512],
                                          start=True, stop=True)
                    ww = [prev_vd] if (half == 0 and prev_vd is not None) else []
                    if half == 0 and ("vd", u) in ydma:
                        ww.append(ydma.pop(("vd", u)))
                    mw = p.emit("tensor", fw, waits=ww, inc="t" if half == 1 else None)
                sa = p.emit("scalar", lambda eng, t=t, c=c, ps_a=ps_a:
                            eng.activation(afac, ps_a, AF.Exp, scale=A[:, c * 16 + t:c * 16 + t + 1]),
                            waits=[ma, vA, prev_vs])
                prev_sa = sa
                vd = p.emit("vector", lambda eng, n0=n0, ps_w=ps_w:
                            eng.tensor_tensor(d1, ps_w, brep[:, n0:n0 + TS], ALU.mult),
                            waits=[mw, bcrep_wait])
                prev_vd = vd
                if u == 0:
                    def fscan(eng, t=t):
                        return eng.tensor_tensor_scan(hsec, afac, d1, state[:, t:t + 1],
                                                      ALU.mult, ALU.add)
                else:
                    def fscan(eng, t=t):
                        return eng.tensor_tensor_scan(hsec, afac, d1, state[:, t:t + 1],
                                                      ALU.mult, ALU.add)
                vs = p.emit("vector", fscan, waits=[sa, zero_state])
                prev_vs = vs
                vst = p.emit("vector", lambda eng, t=t:
                             eng.tensor_copy(state[:, t:t + 1], hsec[:, TS - 1:TS]))
                vh = p.emit("vector", lambda eng, n0=n0:
                            eng.tensor_tensor(hc, hsec, crep[:, n0:n0 + TS], ALU.mult))
                for half in range(2):
                    h0 = half * 512
                    def fr(eng, t=t, h0=h0, ps_y=ps_y, first=(t == 0), last=(t == 15)):
                        return eng.matmul(ps_y[:, h0:h0 + 512],
                                          patred[:, t * 128:(t + 1) * 128],
                                          hc[:, h0:h0 + 512], start=first, stop=last)
                    wr = [vh] if True else []
                    if t == 0 and ("yev", u % 1) in ydma:
                        wr.append(ydma[("yev", 0)])
                    mr = p.emit("tensor", fr, waits=wr, inc="t" if (half == 1 and t == 15) else None)
                red_marks.append(mr)
            sz = p.emit("scalar", lambda eng, n0=n0, zc=zc:
                        eng.activation(zs, zc[:, n0:n0 + TS], AF.Silu),
                        waits=[cw, ydma.get(("zs", 0))])
            v1 = p.emit("vector", lambda eng, n0=n0, c=c, xcc=xcc, ps_y=ps_y:
                        eng.scalar_tensor_tensor(tmpy, xcc[:, n0:n0 + TS], dsk[:, c:c + 1],
                                                 ps_y, ALU.mult, ALU.add),
                        waits=[red_marks[-1], cst_wait])
            yb = yout[u % 2]
            yw = [sz]
            if ("yo", u % 2) in ydma:
                yw.append(ydma[("yo", u % 2)])
            v2 = p.emit("vector", lambda eng, yb=yb: eng.tensor_mul(yb, tmpy, zs), waits=yw)
            ydma[("zs", 0)] = v2
            ydma[("yev", 0)] = v2
            ydma[("sa", (u + 1) % NU)] = prev_vs
            ydma[("vd", (u + 1) % NU)] = prev_vd
            ydma[("yo", u % 2)] = p.dma("sync", o_y[c * 128:(c + 1) * 128, n0:n0 + TS],
                                        yb, waits=[v2], sem=f"yo{u % 2}")
        chunk_done[c] = ("v", p.now("v"))
    p.drain_dmas()
    p.build()
    return p


def l0scan_inputs(inp, dt_all, w_all, xc_all, z_all, bc_all):
    alog = np.asarray(inp["L0_attn_A_log"], np.float32)
    dskip = np.asarray(inp["L0_attn_D_skip"], np.float32)
    patrep = np.zeros((128, 16 * 128), np.float32)
    patred = np.zeros((128, 16 * 128), np.float32)
    for t in range(16):
        for lane in range(128):
            dloc = 8 * t + lane // 16
            patrep[dloc, t * 128 + lane] = 1.0
            patred[lane, t * 128 + dloc] = 1.0
    rs = np.zeros((16, 2 * 128), np.float32)
    for lane in range(128):
        rs[lane % 16, lane] = 1.0
    maps = []
    for b in range(B):
        for dh in range(2):
            rows = slice(dh * 1024, (dh + 1) * 1024)
            asl = alog[rows]
            al = np.zeros((128, 128), np.float32)
            for t in range(128):
                for pp in range(128):
                    al[pp, t] = asl[8 * t + pp // 16, pp % 16]
            dsk = pack_pp(dskip[rows], 8)
            maps.append({
                "i_dt": dt_all[b][rows].astype(BF),
                "i_w": w_all[b][rows].astype(BF),
                "i_xc": xc_all[b][rows].astype(BF),
                "i_z": z_all[b][rows].astype(BF),
                "i_bc": bc_all[b].astype(BF),
                "i_alog": al, "i_dsk": dsk,
                "i_patrep": patrep.astype(BF), "i_patred": patred.astype(BF),
                "i_rs": rs.astype(BF),
            })
    return maps


def ln_stats_resident(p, chunks, TF, ones_a, ones_b, cst_mark, ready_waits=(),
                      banks=(4, 5, 6, 7), eps=1e-5, scr=None):
    ntt = ceil_div(TF, 512)
    assert ntt <= len(banks)
    if scr is None:
        scr = {"xsq": p.f32.alloc(TF), "st2": p.f32.alloc(TF, parts=2),
               "s2c": p.f32.alloc(TF, parts=1), "rstd": p.f32.alloc(TF, parts=1)}
    xsq, st2, s2c, rstd = scr["xsq"], scr["st2"], scr["s2c"], scr["rstd"]
    mu = st2[0:1, :]
    nchunk = len(chunks)
    for c, ch in enumerate(chunks):
        sq = p.emit("scalar", lambda eng, ch=ch: eng.activation(xsq, ch, AF.Square),
                    waits=list(ready_waits))
        tm = None
        for tt in range(ntt):
            n0, n1 = tt * 512, min(tt * 512 + 512, TF)
            bank = p.psum[banks[tt]]
            def f1(eng, ch=ch, n0=n0, n1=n1, bank=bank, first=(c == 0)):
                return eng.matmul(bank[0:2, 0:n1 - n0], ones_a, ch[:, n0:n1],
                                  start=first, stop=False)
            p.emit("tensor", f1, waits=list(ready_waits) + [cst_mark])
            def f2(eng, n0=n0, n1=n1, bank=bank, last=(c == nchunk - 1)):
                return eng.matmul(bank[0:2, 0:n1 - n0], ones_b, xsq[:, n0:n1],
                                  start=False, stop=last)
            tm = p.emit("tensor", f2, waits=[sq], inc="t" if tt == ntt - 1 else None)
        p.emit("scalar", None, waits=[tm])
    tmark = ("t", p.now("t"))
    for tt in range(ntt):
        n0, n1 = tt * 512, min(tt * 512 + 512, TF)
        bank = p.psum[banks[tt]]
        p.emit("vector", lambda eng, n0=n0, n1=n1, bank=bank:
               eng.tensor_copy(st2[:, n0:n1], bank[0:2, 0:n1 - n0]), waits=[tmark])
    inv_n = 1.0 / (nchunk * 128.0)
    p.emit("vector", lambda eng: eng.tensor_scalar_mul(st2[0:2, :], st2[0:2, :], inv_n))
    vm = p.emit("vector", lambda eng: eng.nop())
    dw = p.dma("sync", s2c, st2[1:2, :], waits=[vm], sem="s2c")
    sm = p.emit("scalar", lambda eng: eng.activation(rstd, mu, AF.Square), waits=[vm])
    p.emit("vector", lambda eng: eng.tensor_sub(s2c, s2c, rstd), waits=[sm, dw])
    p.emit("vector", lambda eng: eng.tensor_scalar_add(s2c, s2c, eps))
    vm2 = p.emit("vector", lambda eng: eng.nop())
    sm2 = p.emit("scalar", lambda eng: eng.activation(xsq[0:1, :], s2c, AF.Sqrt),
                 waits=[vm2])
    p.emit("vector", lambda eng: eng.reciprocal(rstd, xsq[0:1, :]), waits=[sm2])
    ready = p.emit("vector", lambda eng: eng.nop())
    return mu, rstd, ready


def ln_apply_resident(p, chunks, mu, rstd, ready, ones1, cst_mark, TF,
                      bank0=0, bank1=1, out_bf=None, lnw=None, lnb=None):
    ntt = ceil_div(TF, 512)
    marks = []
    bc_busy = None
    for tt in range(ntt):
        n0, n1 = tt * 512, min(tt * 512 + 512, TF)
        pm = p.psum[bank0]
        pr = p.psum[bank1]
        bw = [ready, cst_mark] + ([bc_busy] if bc_busy else [])
        def b1(eng, n0=n0, n1=n1, pm=pm):
            return eng.matmul(pm[:, 0:n1 - n0], ones1, mu[0:1, n0:n1],
                              start=True, stop=True)
        p.emit("tensor", b1, waits=bw)
        def b2(eng, n0=n0, n1=n1, pr=pr):
            return eng.matmul(pr[:, 0:n1 - n0], ones1, rstd[0:1, n0:n1],
                              start=True, stop=True)
        tb = p.emit("tensor", b2, inc="t")
        for c, ch in enumerate(chunks):
            def v1(eng, ch=ch, n0=n0, n1=n1, pm=pm):
                return eng.tensor_sub(ch[:, n0:n1], ch[:, n0:n1], pm[:, 0:n1 - n0])
            p.emit("vector", v1, waits=[tb])
            def v2(eng, ch=ch, n0=n0, n1=n1, pr=pr):
                return eng.tensor_mul(ch[:, n0:n1], ch[:, n0:n1], pr[:, 0:n1 - n0])
            m = p.emit("vector", v2)
        bc_busy = m
    for c, ch in enumerate(chunks):
        m = None
        if lnw is not None:
            m = p.emit("vector", lambda eng, c=c, ch=ch:
                       eng.tensor_scalar(ch, ch, lnw[:, c:c + 1], lnb[:, c:c + 1],
                                         ALU.mult, ALU.add))
        if out_bf is not None:
            m = p.emit("vector", lambda eng, c=c, ch=ch:
                       eng.tensor_copy(out_bf[c], ch))
        marks.append(m if m is not None else bc_busy)
    return marks


def build_L0b(nc):
    p = Prog(nc, f32_words=27000, bf16_words=50000)
    i_y = nc.dram_tensor("i_y", [2048, TH], BF16, kind="ExternalInput").ap()
    i_x = nc.dram_tensor("i_x", [D, TH], F32, kind="ExternalInput").ap()
    w_op = nc.dram_tensor("w_op", [2048, D], F32, kind="ExternalInput").ap()
    o_x = nc.dram_tensor("o_x", [D, TH], F32, kind="ExternalOutput").ap()

    ones_a = p.f32.alloc(2)
    ones_b = p.f32.alloc(2)
    ones1 = p.f32.alloc(128, parts=1)
    p.emit("vector", lambda eng: eng.memset(ones_a, 0.0))
    p.emit("vector", lambda eng: eng.memset(ones_a[:, 0:1], 1.0))
    p.emit("vector", lambda eng: eng.memset(ones_b, 0.0))
    p.emit("vector", lambda eng: eng.memset(ones_b[:, 1:2], 1.0))
    p.emit("vector", lambda eng: eng.memset(ones1, 1.0))
    cst = p.emit("vector", lambda eng: eng.nop())

    yb = [p.bf16.alloc(TH) for _ in range(16)]
    yw = None
    for k in range(16):
        yw = p.dma("gpsimd", yb[k], i_y[k * 128:(k + 1) * 128, :], sem="yl")
    acc = [p.f32.alloc(TH) for _ in range(8)]
    xw = None
    for c in range(8):
        xw = p.dma("sync", acc[c], i_x[c * 128:(c + 1) * 128, :], sem="xl")
    slabs, ww = load_weight_slabs(p, w_op, 2048, D)

    def out_cb(m, tt, psum, n0, n1, wait):
        def fe(eng, m=m, n0=n0, n1=n1, psum=psum):
            return eng.tensor_add(acc[m][:, n0:n1], acc[m][:, n0:n1], psum)
        return p.emit("vector", fe, waits=[wait, xw])
    gemm_fm(p, slabs, ww, yb, 8, out_cb, rhs_wait=yw)
    p.barrier()
    mu, rstd, ready = ln_stats_resident(p, acc, TH, ones_a, ones_b, cst)
    marks = ln_apply_resident(p, acc, mu, rstd, ready, ones1, cst, TH)
    for c in range(8):
        p.dma("sync", o_x[c * 128:(c + 1) * 128, :], acc[c], waits=[marks[c]])
    p.drain_dmas()
    p.build()
    return p



_BUILT = {}


def _run(builder, maps, name):
    if name not in _BUILT:
        nc = bass.Bass("TRN2", target_bir_lowering=False)
        builder(nc)
        _BUILT[name] = nc
    res = run_bass_kernel_spmd(_BUILT[name], maps, core_ids=list(range(NCORES)))
    return res.results


def _fm_shards(x):
    out = []
    for b in range(B):
        xt = np.ascontiguousarray(x[b].T)
        for h in range(2):
            out.append(np.ascontiguousarray(xt[:, h * TH:(h + 1) * TH]))
    return out


def _unshard_fm(shards):
    xs = []
    for b in range(B):
        xt = np.concatenate([shards[2 * b], shards[2 * b + 1]], axis=1)
        xs.append(xt.T)
    return np.stack(xs).astype(np.float32)


def run_L0(inp, x):
    maps = l0a_inputs(dict(inp, L0_attn_x=x))
    ra = _run(build_L0a, maps, "L0a")
    dt_all, w_all, xc_all, z_all, bc_all = [], [], [], [], []
    for b in range(B):
        dt_all.append(np.concatenate([np.asarray(ra[2 * b]["o_dt"]),
                                      np.asarray(ra[2 * b + 1]["o_dt"])], axis=1))
        w_all.append(np.concatenate([np.asarray(ra[2 * b]["o_w"]),
                                     np.asarray(ra[2 * b + 1]["o_w"])], axis=1))
        xc_all.append(np.concatenate([np.asarray(ra[2 * b]["o_xc"]),
                                      np.asarray(ra[2 * b + 1]["o_xc"])], axis=1))
        z_all.append(np.concatenate([np.asarray(ra[2 * b]["o_z"]),
                                     np.asarray(ra[2 * b + 1]["o_z"])], axis=1))
        bc_all.append(np.concatenate([np.asarray(ra[2 * b]["o_bc"]),
                                      np.asarray(ra[2 * b + 1]["o_bc"])], axis=1))
    maps = l0scan_inputs(inp, dt_all, w_all, xc_all, z_all, bc_all)
    rs = _run(build_L0scan, maps, "L0scan")
    xf = _fm_shards(x)
    maps3 = []
    for b in range(B):
        yfull = np.concatenate([np.asarray(rs[2 * b]["o_y"]),
                                np.asarray(rs[2 * b + 1]["o_y"])], axis=0)
        for h in range(2):
            maps3.append({"i_y": np.ascontiguousarray(yfull[:, h * TH:(h + 1) * TH]),
                          "i_x": xf[2 * b + h],
                          "w_op": np.ascontiguousarray(inp["L0_attn_out_proj_w"], np.float32)})
    rb = _run(build_L0b, maps3, "L0b")
    return _unshard_fm([np.asarray(r["o_x"], np.float32) for r in rb])


def emit_post_ln_stream(p, acc, x_hbm, o_x, consts, acc_wait, TFt=TH,
                        extra_pass2=None, xsem=("px0", "px1"), nchunk=8):
    ones_a, ones_b, ones1, cst = consts
    ntt = ceil_div(TFt, 512)
    p.f32.mark()
    xrot = [p.f32.alloc(TFt) for _ in range(2)]
    yrot = [p.f32.alloc(TFt) for _ in range(2)]
    xsq = p.f32.alloc(TFt)
    st2 = p.f32.alloc(TFt, parts=2)
    s2c = p.f32.alloc(TFt, parts=1)
    rstd = p.f32.alloc(TFt, parts=1)
    mu = st2[0:1, :]
    cons = {}
    banks = (4, 5, 6, 7)
    for c in range(nchunk):
        xbuf = xrot[c % 2]
        ybuf = yrot[c % 2]
        waits = [cons[c - 2]] if c - 2 in cons else [acc_wait]
        w = p.dma("sync", xbuf, x_hbm[c * 128:(c + 1) * 128, :], waits=waits,
                  sem=xsem[c % 2])
        va = p.emit("vector", lambda eng, xbuf=xbuf, ybuf=ybuf, c=c:
                    eng.tensor_add(ybuf, xbuf, acc[c]), waits=[w, acc_wait])
        sq = p.emit("scalar", lambda eng, ybuf=ybuf: eng.activation(xsq, ybuf, AF.Square),
                    waits=[va])
        tm = None
        for tt in range(ntt):
            n0, n1 = tt * 512, min(tt * 512 + 512, TFt)
            bank = p.psum[banks[tt]]
            def f1(eng, ybuf=ybuf, n0=n0, n1=n1, bank=bank, first=(c == 0)):
                return eng.matmul(bank[0:2, 0:n1 - n0], ones_a, ybuf[:, n0:n1],
                                  start=first, stop=False)
            p.emit("tensor", f1, waits=[va, cst])
            def f2(eng, n0=n0, n1=n1, bank=bank, last=(c == nchunk - 1)):
                return eng.matmul(bank[0:2, 0:n1 - n0], ones_b, xsq[:, n0:n1],
                                  start=False, stop=last)
            tm = p.emit("tensor", f2, waits=[sq], inc="t" if tt == ntt - 1 else None)
        cons[c] = tm
        p.emit("scalar", None, waits=[tm])
    for tt in range(ntt):
        n0, n1 = tt * 512, min(tt * 512 + 512, TFt)
        bank = p.psum[banks[tt]]
        p.emit("vector", lambda eng, n0=n0, n1=n1, bank=bank:
               eng.tensor_copy(st2[:, n0:n1], bank[0:2, 0:n1 - n0]),
               waits=[("t", p.now("t"))])
    inv_n = 1.0 / (nchunk * 128.0)
    p.emit("vector", lambda eng: eng.tensor_scalar_mul(st2[0:2, :], st2[0:2, :], inv_n))
    vm = p.emit("vector", lambda eng: eng.nop())
    dwx = p.dma("sync", s2c, st2[1:2, :], waits=[vm], sem="s2c")
    sm = p.emit("scalar", lambda eng: eng.activation(rstd, mu, AF.Square), waits=[vm])
    p.emit("vector", lambda eng: eng.tensor_sub(s2c, s2c, rstd), waits=[sm, dwx])
    p.emit("vector", lambda eng: eng.tensor_scalar_add(s2c, s2c, 1e-5))
    vm2 = p.emit("vector", lambda eng: eng.nop())
    sm2 = p.emit("scalar", lambda eng: eng.activation(xsq[0:1, :], s2c, AF.Sqrt),
                 waits=[vm2])
    rr = p.emit("vector", lambda eng: eng.reciprocal(rstd, xsq[0:1, :]), waits=[sm2])
    cons2 = {}
    out_marks = []
    bc_busy = None
    for c in range(nchunk):
        xbuf = xrot[c % 2]
        ybuf = yrot[c % 2]
        waits = [cons2[c - 2]] if c - 2 in cons2 else [cons[nchunk - 2 + (c % 2)]]
        w = p.dma("sync", xbuf, x_hbm[c * 128:(c + 1) * 128, :], waits=waits,
                  sem=xsem[c % 2])
        va = p.emit("vector", lambda eng, xbuf=xbuf, ybuf=ybuf, c=c:
                    eng.tensor_add(ybuf, xbuf, acc[c]), waits=[w, rr])
        for tt in range(ntt):
            n0, n1 = tt * 512, min(tt * 512 + 512, TFt)
            pm, pr = p.psum[0], p.psum[1]
            def b1(eng, n0=n0, n1=n1, pm=pm):
                return eng.matmul(pm[:, 0:n1 - n0], ones1, mu[0:1, n0:n1],
                                  start=True, stop=True)
            p.emit("tensor", b1, waits=[rr, cst] + ([bc_busy] if bc_busy else []))
            def b2(eng, n0=n0, n1=n1, pr=pr):
                return eng.matmul(pr[:, 0:n1 - n0], ones1, rstd[0:1, n0:n1],
                                  start=True, stop=True)
            tb = p.emit("tensor", b2, inc="t")
            p.emit("vector", lambda eng, ybuf=ybuf, n0=n0, n1=n1, pm=pm:
                   eng.tensor_sub(ybuf[:, n0:n1], ybuf[:, n0:n1], pm[:, 0:n1 - n0]),
                   waits=[tb])
            bc_busy = p.emit("vector", lambda eng, ybuf=ybuf, n0=n0, n1=n1, pr=pr:
                             eng.tensor_mul(ybuf[:, n0:n1], ybuf[:, n0:n1],
                                            pr[:, 0:n1 - n0]))
        mark = bc_busy
        if extra_pass2 is not None:
            mark = extra_pass2(c, ybuf, mark)
        dm = p.dma("sync", o_x[c * 128:(c + 1) * 128, :], ybuf, waits=[mark],
                   sem=f"pox{c % 2}")
        cons2[c] = dm
        out_marks.append(mark)
    p.f32.reset()
    return out_marks


def alloc_consts(p):
    ones_a = p.f32.alloc(2)
    ones_b = p.f32.alloc(2)
    ones1 = p.f32.alloc(128, parts=1)
    p.emit("vector", lambda eng: eng.memset(ones_a, 0.0))
    p.emit("vector", lambda eng: eng.memset(ones_a[:, 0:1], 1.0))
    p.emit("vector", lambda eng: eng.memset(ones_b, 0.0))
    p.emit("vector", lambda eng: eng.memset(ones_b[:, 1:2], 1.0))
    p.emit("vector", lambda eng: eng.memset(ones1, 1.0))
    cst = p.emit("vector", lambda eng: eng.nop())
    return (ones_a, ones_b, ones1, cst)


def build_L0ffn(nc):
    TW = 1024
    p = Prog(nc, f32_words=21000, bf16_words=52000)
    i_x = nc.dram_tensor("i_x", [D, TW], F32, kind="ExternalInput").ap()
    w_rt = nc.dram_tensor("w_rt", [D, 14], F32, kind="ExternalInput").ap()
    w1 = nc.dram_tensor("w1", [14, D, 2048], F32, kind="ExternalInput").ap()
    w2 = nc.dram_tensor("w2", [14, 2048, D], F32, kind="ExternalInput").ap()
    b1 = nc.dram_tensor("b1", [128, 14 * 16], F32, kind="ExternalInput").ap()
    b2t = nc.dram_tensor("b2t", [14, D], F32, kind="ExternalInput").ap()
    ident = nc.dram_tensor("ident", [128, 128], F32, kind="ExternalInput").ap()
    w_l1r = nc.dram_tensor("w_l1r", [128, 8], F32, kind="ExternalInput").ap()
    b_l1r = nc.dram_tensor("b_l1r", [1, 1], F32, kind="ExternalInput").ap()
    o_x = nc.dram_tensor("o_x", [D, TW], F32, kind="ExternalOutput").ap()
    o_rw = nc.dram_tensor("o_rw", [1, TW], F32, kind="ExternalOutput").ap()
    gate_scr = nc.dram_tensor("gate_scr", [14, TW], F32, kind="Internal").ap()

    consts = alloc_consts(p)
    ones_a, ones_b, ones1, cst = consts
    NTT = TW // 512

    idb = p.f32.alloc(128)
    b1sb = p.f32.alloc(14 * 16)
    l1r = p.f32.alloc(8)
    l1rb = p.f32.alloc(1, parts=1)
    p.dma("sync", idb, ident)
    p.dma("sync", b1sb, b1)
    p.dma("sync", l1r, w_l1r)
    p.dma("sync", l1rb, b_l1r)
    prm = ("dS", p.now("dS"))
    b2sb = p.bf16.alloc(D, parts=14)
    b2w = p.dma("gpsimd", b2sb, b2t, sem="b2l")
    grow = p.f32.alloc(TW, parts=1)
    rw = p.f32.alloc(TW, parts=1)

    xb = [p.bf16.alloc(TW) for _ in range(8)]
    acc = [p.bf16.alloc(TW) for _ in range(8)]
    hid = [p.bf16.alloc(TW) for _ in range(8)]
    grep = [p.bf16.alloc(TW) for _ in range(2)]
    gmat = p.bf16.alloc(TW, parts=14)

    rt, rtw = load_weight_slabs(p, w_rt, D, 14, dtype=F32, queue="sync")
    p.f32.mark()
    xrot = [p.f32.alloc(TW) for _ in range(2)]
    lg = p.f32.alloc(TW, parts=14)
    cdone = {}
    tmk = None
    for c in range(8):
        waits = list(cdone[c - 2]) if c - 2 in cdone else []
        w = p.dma("sync", xrot[c % 2], i_x[c * 128:(c + 1) * 128, :], waits=waits,
                  sem=f"xl{c % 2}")
        cp = p.emit("vector", lambda eng, c=c: eng.tensor_copy(xb[c], xrot[c % 2]),
                    waits=[w])
        for tt in range(NTT):
            n0 = tt * 512
            psum = p.psum[tt % 2]
            def fm(eng, c=c, n0=n0, psum=psum, first=(c == 0), last=(c == 7)):
                return eng.matmul(psum[0:14, 0:512], rt[c], xrot[c % 2][:, n0:n0 + 512],
                                  start=first, stop=last)
            tmk = p.emit("tensor", fm, waits=[w, rtw], inc="t")
        cdone[c] = (cp, tmk)
    xb_wait = ("v", p.now("v"))
    eb = {}
    for tt in range(NTT):
        n0 = tt * 512
        psum = p.psum[tt % 2]
        eb[tt % 2] = p.emit("vector", lambda eng, n0=n0, psum=psum:
                            eng.tensor_copy(lg[:, n0:n0 + 512], psum[0:14, 0:512]),
                            waits=[tmk])
    lg_wait = ("v", p.now("v"))
    idw = prm
    lgt = p.f32.alloc(16)
    gt = p.f32.alloc(16)
    mx = p.f32.alloc(1)
    sm_ = p.f32.alloc(1)
    gfm = p.f32.alloc(128, parts=14)
    prev = None
    gdma = None
    for blk in range(TW // 128):
        def ftr(eng, blk=blk):
            return eng.transpose(p.psum[2][0:128, 0:14], lg[:, blk * 128:(blk + 1) * 128],
                                 idb[0:14, 0:14])
        mt = p.emit("tensor", ftr, waits=[lg_wait, idw] + ([prev] if prev else []), inc="t")
        ev = p.emit("vector", lambda eng: eng.tensor_copy(lgt[:, 0:14], p.psum[2][0:128, 0:14]),
                    waits=[mt])
        p.emit("vector", lambda eng: eng.reduce_max(mx, lgt[:, 0:14], axis=AX.X))
        vm = p.emit("vector", lambda eng: eng.tensor_scalar_mul(mx, mx, -1.0))
        se = p.emit("scalar", lambda eng: eng.activation(lgt[:, 0:14], lgt[:, 0:14],
                                                         AF.Exp, bias=mx, accum_out=sm_),
                    waits=[vm])
        rc = p.emit("vector", lambda eng: eng.reciprocal(sm_, sm_), waits=[se])
        p.emit("vector", lambda eng: eng.tensor_scalar_mul(lgt[:, 0:14], lgt[:, 0:14], sm_))
        p.emit("vector", lambda eng: eng.reduce_max(mx, lgt[:, 0:14], axis=AX.X))
        p.emit("vector", lambda eng: eng.tensor_tensor(gt[:, 0:14], lgt[:, 0:14],
                                                       mx.broadcast_to([128, 14]), ALU.is_equal))
        vg = p.emit("vector", lambda eng: eng.tensor_tensor(gt[:, 0:14], gt[:, 0:14],
                                                            mx.broadcast_to([128, 14]), ALU.mult))
        def ftr2(eng, blk=blk):
            return eng.transpose(p.psum[3][0:14, 0:128], gt[:, 0:14], idb)
        mt2 = p.emit("tensor", ftr2, waits=[vg], inc="t")
        prev = p.emit("vector", lambda eng: eng.tensor_copy(gfm, p.psum[3][0:14, 0:128]),
                      waits=[mt2] + ([gdma] if gdma else []))
        gdma = p.dma("sync", gate_scr[:, blk * 128:(blk + 1) * 128], gfm, waits=[prev],
                     sem="gsc")
    gsc_done = ("gsc", p.now("gsc"))
    p.barrier()

    p.bf16.mark()
    ctx = {}
    gm_w = p.dma("gpsimd", gmat, gate_scr, waits=[gsc_done], sem="gmr")
    acc_zero = []
    for c in range(8):
        acc_zero.append(p.emit("vector", lambda eng, c=c: eng.memset(acc[c], 0.0)))
    grow_busy = None
    ps4_busy = None
    for e in range(14):
        gw = p.dma("sync", grow, gate_scr[e:e + 1, :],
                   waits=[gsc_done] + ([grow_busy] if grow_busy else []), sem="grw")
        gb = grep[e % 2]
        mk = None
        mb = None
        for tt in range(NTT):
            n0 = tt * 512
            def fb(eng, n0=n0):
                return eng.matmul(p.psum[4][:, 0:512], ones1, grow[0:1, n0:n0 + 512],
                                  start=True, stop=True)
            mb = p.emit("tensor", fb, waits=[gw, cst] + ([mk] if mk else []), inc="t")
            mk = p.emit("vector", lambda eng, n0=n0, gb=gb:
                        eng.tensor_copy(gb[:, n0:n0 + 512], p.psum[4][:, 0:512]),
                        waits=[mb])
        grow_busy = mb
        grep_wait = mk

        for half in range(2):
            f0 = half * 8
            def out_cb(m, tt, psum, n0, n1, wait, e=e, f0=f0):
                def fe(eng, m=m, n0=n0, n1=n1, psum=psum, e=e):
                    return eng.activation(hid[m - f0][:, n0:n1], psum[:, 0:n1 - n0],
                                          AF.Gelu, bias=b1sb[:, e * 16 + m:e * 16 + m + 1])
                sm1 = p.emit("scalar", fe, waits=[wait, prm])
                def fg(eng, m=m, n0=n0, n1=n1, e=e, f0=f0):
                    return eng.tensor_mul(hid[m - f0][:, n0:n1], hid[m - f0][:, n0:n1],
                                          grep[e % 2][:, n0:n1])
                return p.emit("vector", fg, waits=[sm1, grep_wait])
            gemm_cols(p, w1[e], D, xb, xb_wait, list(range(f0, f0 + 8)), out_cb,
                      group_mt=4, ctx=ctx)
            hid_wait = ("v", p.now("v"))
            def out_cb2(m, tt, psum, n0, n1, wait):
                def fa(eng, m=m, n0=n0, n1=n1, psum=psum):
                    return eng.tensor_add(acc[m][:, n0:n1], acc[m][:, n0:n1],
                                          psum[:, 0:n1 - n0])
                return p.emit("vector", fa, waits=[wait] + acc_zero)
            gemm_cols(p, w2[e][f0 * 128:(f0 + 8) * 128, :], 1024, hid, hid_wait,
                      list(range(8)), out_cb2, group_mt=4, ctx=ctx)
    bm = {0: ctx["psum_busy"].get(2), 1: ctx["psum_busy"].get(3)}
    for tt in range(NTT):
        n0 = tt * 512
        for m in range(8):
            psum = p.psum[(m + tt) % 2 + 2]
            pb = bm.get((m + tt) % 2)
            def fb2(eng, m=m, n0=n0, psum=psum):
                return eng.matmul(psum[:, 0:512], b2sb[:, m * 128:(m + 1) * 128],
                                  gmat[:, n0:n0 + 512], start=True, stop=True)
            mbx = p.emit("tensor", fb2, waits=[b2w, gm_w] + ([pb] if pb else []), inc="t")
            bm[(m + tt) % 2] = p.emit("vector", lambda eng, m=m, n0=n0, psum=psum:
                                      eng.tensor_add(acc[m][:, n0:n0 + 512],
                                                     acc[m][:, n0:n0 + 512],
                                                     psum[:, 0:512]), waits=[mbx])
    acc_wait = ("v", p.now("v"))
    p.barrier()
    p.bf16.reset()
    p.f32.reset()

    rwacc = {}
    def extra(c, ych, mark):
        mm = None
        for tt in range(NTT):
            n0 = tt * 512
            def fr(eng, c=c, n0=n0, tt=tt, ych=ych):
                return eng.matmul(p.psum[2 + tt][0:1, 0:512], l1r[:, c:c + 1],
                                  ych[:, n0:n0 + 512], start=(c == 0), stop=(c == 7))
            mm = p.emit("tensor", fr, waits=[mark, prm], inc="t")
            rwacc[tt] = mm
        return mm
    marks = emit_post_ln_stream(p, acc, i_x, o_x, consts, acc_wait, TFt=TW,
                                extra_pass2=extra)
    for tt in range(NTT):
        n0 = tt * 512
        p.emit("vector", lambda eng, n0=n0, tt=tt:
               eng.tensor_scalar_add(rw[0:1, n0:n0 + 512], p.psum[2 + tt][0:1, 0:512],
                                     l1rb[0:1, 0:1]), waits=[rwacc[tt]])
    vrw = p.emit("vector", lambda eng: eng.nop())
    p.dma("sync", o_rw, rw, waits=[vrw])
    p.drain_dmas()
    p.build()
    return p



_BUILT = {}


def _run(builder, maps, name):
    if name not in _BUILT:
        nc = bass.Bass("TRN2", target_bir_lowering=False)
        builder(nc)
        _BUILT[name] = nc
    res = run_bass_kernel_spmd(_BUILT[name], maps, core_ids=list(range(NCORES)))
    return res.results


def _fm_shards(x):
    out = []
    for b in range(B):
        xt = np.ascontiguousarray(x[b].T)
        for h in range(2):
            out.append(np.ascontiguousarray(xt[:, h * TH:(h + 1) * TH]))
    return out


def _unshard_fm(shards):
    xs = []
    for b in range(B):
        xt = np.concatenate([shards[2 * b], shards[2 * b + 1]], axis=1)
        xs.append(xt.T)
    return np.stack(xs).astype(np.float32)


def run_L0(inp, x):
    maps = l0a_inputs(dict(inp, L0_attn_x=x))
    ra = _run(build_L0a, maps, "L0a")
    dt_all, w_all, xc_all, z_all, bc_all = [], [], [], [], []
    for b in range(B):
        dt_all.append(np.concatenate([np.asarray(ra[2 * b]["o_dt"]),
                                      np.asarray(ra[2 * b + 1]["o_dt"])], axis=1))
        w_all.append(np.concatenate([np.asarray(ra[2 * b]["o_w"]),
                                     np.asarray(ra[2 * b + 1]["o_w"])], axis=1))
        xc_all.append(np.concatenate([np.asarray(ra[2 * b]["o_xc"]),
                                      np.asarray(ra[2 * b + 1]["o_xc"])], axis=1))
        z_all.append(np.concatenate([np.asarray(ra[2 * b]["o_z"]),
                                     np.asarray(ra[2 * b + 1]["o_z"])], axis=1))
        bc_all.append(np.concatenate([np.asarray(ra[2 * b]["o_bc"]),
                                      np.asarray(ra[2 * b + 1]["o_bc"])], axis=1))
    maps = l0scan_inputs(inp, dt_all, w_all, xc_all, z_all, bc_all)
    rs = _run(build_L0scan, maps, "L0scan")
    xf = _fm_shards(x)
    maps3 = []
    for b in range(B):
        yfull = np.concatenate([np.asarray(rs[2 * b]["o_y"]),
                                np.asarray(rs[2 * b + 1]["o_y"])], axis=0)
        for h in range(2):
            maps3.append({"i_y": np.ascontiguousarray(yfull[:, h * TH:(h + 1) * TH]),
                          "i_x": xf[2 * b + h],
                          "w_op": np.ascontiguousarray(inp["L0_attn_out_proj_w"], np.float32)})
    rb = _run(build_L0b, maps3, "L0b")
    return _unshard_fm([np.asarray(r["o_x"], np.float32) for r in rb])


def emit_post_ln_stream(p, acc, x_hbm, o_x, consts, acc_wait, TFt=TH,
                        extra_pass2=None, xsem=("px0", "px1"), nchunk=8):
    ones_a, ones_b, ones1, cst = consts
    ntt = ceil_div(TFt, 512)
    p.f32.mark()
    xrot = [p.f32.alloc(TFt) for _ in range(2)]
    yrot = [p.f32.alloc(TFt) for _ in range(2)]
    xsq = p.f32.alloc(TFt)
    st2 = p.f32.alloc(TFt, parts=2)
    s2c = p.f32.alloc(TFt, parts=1)
    rstd = p.f32.alloc(TFt, parts=1)
    mu = st2[0:1, :]
    cons = {}
    banks = (4, 5, 6, 7)
    for c in range(nchunk):
        xbuf = xrot[c % 2]
        ybuf = yrot[c % 2]
        waits = [cons[c - 2]] if c - 2 in cons else [acc_wait]
        w = p.dma("sync", xbuf, x_hbm[c * 128:(c + 1) * 128, :], waits=waits,
                  sem=xsem[c % 2])
        va = p.emit("vector", lambda eng, xbuf=xbuf, ybuf=ybuf, c=c:
                    eng.tensor_add(ybuf, xbuf, acc[c]), waits=[w, acc_wait])
        sq = p.emit("scalar", lambda eng, ybuf=ybuf: eng.activation(xsq, ybuf, AF.Square),
                    waits=[va])
        tm = None
        for tt in range(ntt):
            n0, n1 = tt * 512, min(tt * 512 + 512, TFt)
            bank = p.psum[banks[tt]]
            def f1(eng, ybuf=ybuf, n0=n0, n1=n1, bank=bank, first=(c == 0)):
                return eng.matmul(bank[0:2, 0:n1 - n0], ones_a, ybuf[:, n0:n1],
                                  start=first, stop=False)
            p.emit("tensor", f1, waits=[va, cst])
            def f2(eng, n0=n0, n1=n1, bank=bank, last=(c == nchunk - 1)):
                return eng.matmul(bank[0:2, 0:n1 - n0], ones_b, xsq[:, n0:n1],
                                  start=False, stop=last)
            tm = p.emit("tensor", f2, waits=[sq], inc="t" if tt == ntt - 1 else None)
        cons[c] = tm
        p.emit("scalar", None, waits=[tm])
    for tt in range(ntt):
        n0, n1 = tt * 512, min(tt * 512 + 512, TFt)
        bank = p.psum[banks[tt]]
        p.emit("vector", lambda eng, n0=n0, n1=n1, bank=bank:
               eng.tensor_copy(st2[:, n0:n1], bank[0:2, 0:n1 - n0]),
               waits=[("t", p.now("t"))])
    inv_n = 1.0 / (nchunk * 128.0)
    p.emit("vector", lambda eng: eng.tensor_scalar_mul(st2[0:2, :], st2[0:2, :], inv_n))
    vm = p.emit("vector", lambda eng: eng.nop())
    dwx = p.dma("sync", s2c, st2[1:2, :], waits=[vm], sem="s2c")
    sm = p.emit("scalar", lambda eng: eng.activation(rstd, mu, AF.Square), waits=[vm])
    p.emit("vector", lambda eng: eng.tensor_sub(s2c, s2c, rstd), waits=[sm, dwx])
    p.emit("vector", lambda eng: eng.tensor_scalar_add(s2c, s2c, 1e-5))
    vm2 = p.emit("vector", lambda eng: eng.nop())
    sm2 = p.emit("scalar", lambda eng: eng.activation(xsq[0:1, :], s2c, AF.Sqrt),
                 waits=[vm2])
    rr = p.emit("vector", lambda eng: eng.reciprocal(rstd, xsq[0:1, :]), waits=[sm2])
    cons2 = {}
    out_marks = []
    bc_busy = None
    for c in range(nchunk):
        xbuf = xrot[c % 2]
        ybuf = yrot[c % 2]
        waits = [cons2[c - 2]] if c - 2 in cons2 else [cons[nchunk - 2 + (c % 2)]]
        w = p.dma("sync", xbuf, x_hbm[c * 128:(c + 1) * 128, :], waits=waits,
                  sem=xsem[c % 2])
        va = p.emit("vector", lambda eng, xbuf=xbuf, ybuf=ybuf, c=c:
                    eng.tensor_add(ybuf, xbuf, acc[c]), waits=[w, rr])
        for tt in range(ntt):
            n0, n1 = tt * 512, min(tt * 512 + 512, TFt)
            pm, pr = p.psum[0], p.psum[1]
            def b1(eng, n0=n0, n1=n1, pm=pm):
                return eng.matmul(pm[:, 0:n1 - n0], ones1, mu[0:1, n0:n1],
                                  start=True, stop=True)
            p.emit("tensor", b1, waits=[rr, cst] + ([bc_busy] if bc_busy else []))
            def b2(eng, n0=n0, n1=n1, pr=pr):
                return eng.matmul(pr[:, 0:n1 - n0], ones1, rstd[0:1, n0:n1],
                                  start=True, stop=True)
            tb = p.emit("tensor", b2, inc="t")
            p.emit("vector", lambda eng, ybuf=ybuf, n0=n0, n1=n1, pm=pm:
                   eng.tensor_sub(ybuf[:, n0:n1], ybuf[:, n0:n1], pm[:, 0:n1 - n0]),
                   waits=[tb])
            bc_busy = p.emit("vector", lambda eng, ybuf=ybuf, n0=n0, n1=n1, pr=pr:
                             eng.tensor_mul(ybuf[:, n0:n1], ybuf[:, n0:n1],
                                            pr[:, 0:n1 - n0]))
        mark = bc_busy
        if extra_pass2 is not None:
            mark = extra_pass2(c, ybuf, mark)
        dm = p.dma("sync", o_x[c * 128:(c + 1) * 128, :], ybuf, waits=[mark],
                   sem=f"pox{c % 2}")
        cons2[c] = dm
        out_marks.append(mark)
    p.f32.reset()
    return out_marks


def alloc_consts(p):
    ones_a = p.f32.alloc(2)
    ones_b = p.f32.alloc(2)
    ones1 = p.f32.alloc(128, parts=1)
    p.emit("vector", lambda eng: eng.memset(ones_a, 0.0))
    p.emit("vector", lambda eng: eng.memset(ones_a[:, 0:1], 1.0))
    p.emit("vector", lambda eng: eng.memset(ones_b, 0.0))
    p.emit("vector", lambda eng: eng.memset(ones_b[:, 1:2], 1.0))
    p.emit("vector", lambda eng: eng.memset(ones1, 1.0))
    cst = p.emit("vector", lambda eng: eng.nop())
    return (ones_a, ones_b, ones1, cst)


def build_L1attn(nc):
    p = Prog(nc, f32_words=38000, bf16_words=8000)
    i_x = nc.dram_tensor("i_x", [D, TH], F32, kind="ExternalInput").ap()
    i_xrev = nc.dram_tensor("i_xrev", [D, TH], F32, kind="ExternalInput").ap()
    i_rw = nc.dram_tensor("i_rw", [128, 32], F32, kind="ExternalInput").ap()
    i_rwown = nc.dram_tensor("i_rwown", [1, TH], F32, kind="ExternalInput").ap()
    o_x = nc.dram_tensor("o_x", [D, TH], F32, kind="ExternalOutput").ap()
    o_dbg = nc.dram_tensor("o_dbg", [1, 5], F32, kind="ExternalOutput").ap()

    consts = alloc_consts(p)
    ones_a, ones_b, ones1, cst = consts
    ones128c = p.f32.alloc(1)
    p.emit("vector", lambda eng: eng.memset(ones128c, 1.0))
    cst2 = p.emit("vector", lambda eng: eng.nop())

    rwrs = p.f32.alloc(32)
    p.dma("sync", rwrs, i_rw)
    rwown = p.f32.alloc(TH, parts=1)
    p.dma("sync", rwown, i_rwown)
    rww = rwo_w = ("dS", p.now("dS"))

    mm2 = p.f32.alloc(2)
    sc = p.f32.alloc(8, parts=1)
    lo, hi, Tm, pred, tmp1, notp = (sc[0:1, i:i + 1] for i in range(6))
    cnt = p.f32.alloc(1)
    ge = p.f32.alloc(32)
    idb = p.f32.alloc(128)
    iot_r = p.i32 and None
    mscr = nc.dram_tensor("mscr", [2, 128], F32, kind="Internal").ap()
    p.emit("vector", lambda eng: eng.reduce_max(mm2[:, 0:1], rwrs, axis=AX.X), waits=[rww])
    p.emit("vector", lambda eng: eng.tensor_reduce(mm2[:, 1:2], rwrs, AX.X, ALU.min),
           waits=[rww])
    vv = p.emit("vector", lambda eng: eng.nop())
    p.dma("sync", mscr[0:1, :].rearrange("one p -> p one"), mm2[:, 0:1], waits=[vv], sem="msc")
    dsc = p.dma("sync", mscr[1:2, :].rearrange("one p -> p one"), mm2[:, 1:2], waits=[vv], sem="msc")
    row2 = p.f32.alloc(256, parts=1)
    dsc2 = p.dma("sync", row2, mscr.rearrange("two p -> (two p)")[None, :], waits=[dsc], sem="msc")
    p.emit("vector", lambda eng: eng.reduce_max(hi, row2[0:1, 0:128], axis=AX.X), waits=[dsc2])
    p.emit("vector", lambda eng: eng.tensor_reduce(lo, row2[0:1, 128:256], AX.X, ALU.min),
           waits=[dsc2])
    p.emit("vector", lambda eng: eng.tensor_scalar_add(hi, hi, 1.0))
    p.emit("vector", lambda eng: eng.tensor_scalar_add(lo, lo, -1.0))
    for it in range(45):
        p.emit("vector", lambda eng: eng.tensor_add(Tm, lo, hi))
        vT = p.emit("vector", lambda eng: eng.tensor_scalar_mul(Tm, Tm, 0.5))
        mb = p.emit("tensor", lambda eng: eng.matmul(p.psum[0][:, 0:1], ones1, Tm,
                                                     start=True, stop=True),
                    waits=[vT, cst], inc="t")
        Trep = p.f32.alloc(1) if False else None
        vg = p.emit("vector", lambda eng: eng.tensor_scalar(ge, rwrs, p.psum[0][0:128, 0:1],
                                                            0.0, ALU.is_ge, ALU.add,
                                                            accum_out=cnt), waits=[mb])
        mc = p.emit("tensor", lambda eng: eng.matmul(p.psum[1][0:1, 0:1], cnt, ones128c,
                                                     start=True, stop=True),
                    waits=[vg, cst2], inc="t")
        vp = p.emit("vector", lambda eng: eng.tensor_scalar(pred, p.psum[1][0:1, 0:1], 1024.0,
                                                            None, ALU.is_ge, ALU.bypass),
                    waits=[mc])
        p.emit("vector", lambda eng: eng.tensor_sub(tmp1, Tm, lo))
        p.emit("vector", lambda eng: eng.tensor_mul(tmp1, tmp1, pred))
        p.emit("vector", lambda eng: eng.tensor_add(lo, lo, tmp1))
        p.emit("vector", lambda eng: eng.tensor_sub(tmp1, hi, Tm))
        p.emit("vector", lambda eng: eng.tensor_scalar(notp, pred, -1.0, 1.0,
                                                       ALU.mult, ALU.add))
        p.emit("vector", lambda eng: eng.tensor_mul(tmp1, tmp1, notp))
        p.emit("vector", lambda eng: eng.tensor_sub(hi, hi, tmp1))
    p.dma("sync", o_dbg, sc[0:1, 0:5], waits=[("v", p.now("v"))], sem="dbg")
    mrow = p.f32.alloc(TH, parts=1)
    vm = p.emit("vector", lambda eng: eng.tensor_scalar(mrow, rwown, lo, None,
                                                        ALU.is_ge, ALU.bypass),
                waits=[rwo_w])
    acc = [p.f32.alloc(TH) for _ in range(8)]
    xrv = [None] * 2
    p.f32.mark()
    xrot = [p.f32.alloc(TH) for _ in range(2)]
    cd = {}
    for c in range(8):
        waits = [cd[c - 2]] if c - 2 in cd else []
        w = p.dma("sync", xrot[c % 2], i_xrev[c * 128:(c + 1) * 128, :], waits=waits,
                  sem=f"xr{c % 2}")
        prev_vd = cd.get("pvd")
        for tt in range(4):
            n0 = tt * 512
            mb = p.emit("tensor", lambda eng, n0=n0: eng.matmul(p.psum[2][:, 0:512], ones1,
                                                                mrow[0:1, n0:n0 + 512],
                                                                start=True, stop=True),
                        waits=[vm, cst] + ([prev_vd] if prev_vd else []), inc="t")
            prev_vd = p.emit("vector", lambda eng, c=c, n0=n0:
                             eng.tensor_tensor(acc[c][:, n0:n0 + 512],
                                               xrot[c % 2][:, n0:n0 + 512],
                                               p.psum[2][:, 0:512], ALU.mult),
                             waits=[mb, w])
        cd["pvd"] = prev_vd
        cd[c] = prev_vd
    acc_wait = ("v", p.now("v"))
    p.f32.reset()
    emit_post_ln_stream(p, acc, i_x, o_x, consts, acc_wait)
    p.drain_dmas()
    p.build()
    return p


def run_L0ffn(inp, x1):
    common = {
        "w_rt": np.ascontiguousarray(inp['L0_ffn_router_w'], np.float32),
        "w1": np.ascontiguousarray(inp['L0_ffn_w1'], np.float32),
        "w2": np.ascontiguousarray(inp['L0_ffn_w2'], np.float32),
        "b1": np.ascontiguousarray(np.asarray(inp['L0_ffn_b1'], np.float32)
                                   .reshape(14, 16, 128).transpose(2, 0, 1).reshape(128, 224)),
        "b2t": np.ascontiguousarray(inp['L0_ffn_b2'], np.float32),
        "ident": np.eye(128, dtype=np.float32),
        "w_l1r": pack_pp(np.asarray(inp['L1_attn_router_w'], np.float32)[0], 8),
        "b_l1r": np.asarray(inp['L1_attn_router_b'], np.float32).reshape(1, 1),
    }
    x2 = np.empty((B, T, D), np.float32)
    rw = np.empty((B, T), np.float32)
    for q in range(2):
        maps = []
        for core in range(8):
            b, h = core // 2, core % 2
            t0 = h * TH + q * 1024
            m = dict(common)
            m["i_x"] = np.ascontiguousarray(x1[b].T[:, t0:t0 + 1024])
            maps.append(m)
        res = _run(build_L0ffn, maps, "L0ffn")
        for core in range(8):
            b, h = core // 2, core % 2
            t0 = h * TH + q * 1024
            x2[b, t0:t0 + 1024] = np.asarray(res[core]["o_x"], np.float32).T
            rw[b, t0:t0 + 1024] = np.asarray(res[core]["o_rw"], np.float32)[0]
    return x2, rw


def run_L1attn(inp, x2, rw):
    maps = []
    for core in range(8):
        b, h = core // 2, core % 2
        t0 = h * TH
        xt = x2[b].T
        xrev = xt[:, ::-1][:, t0:t0 + TH]
        maps.append({
            "i_x": np.ascontiguousarray(xt[:, t0:t0 + TH]),
            "i_xrev": np.ascontiguousarray(xrev),
            "i_rw": np.ascontiguousarray(rw[b].reshape(32, 128).T),
            "i_rwown": np.ascontiguousarray(rw[b][t0:t0 + TH].reshape(1, TH)),
        })
    res = _run(build_L1attn, maps, "L1attn")
    return _unshard_fm([np.asarray(r["o_x"], np.float32) for r in res])


def build_L1ffn(nc):
    TW = 1024
    p = Prog(nc, f32_words=21000, bf16_words=52000)
    i_x = nc.dram_tensor("i_x", [D, TW], F32, kind="ExternalInput").ap()
    w_gt = nc.dram_tensor("w_gt", [D, 8], F32, kind="ExternalInput").ap()
    wg = nc.dram_tensor("wg", [8, D, 2048], F32, kind="ExternalInput").ap()
    wu = nc.dram_tensor("wu", [8, D, 2048], F32, kind="ExternalInput").ap()
    wd = nc.dram_tensor("wd", [8, 2048, D], F32, kind="ExternalInput").ap()
    wsg = nc.dram_tensor("wsg", [D, 2048], F32, kind="ExternalInput").ap()
    wsu = nc.dram_tensor("wsu", [D, 2048], F32, kind="ExternalInput").ap()
    wsd = nc.dram_tensor("wsd", [2048, D], F32, kind="ExternalInput").ap()
    ident = nc.dram_tensor("ident", [128, 128], F32, kind="ExternalInput").ap()
    o_x = nc.dram_tensor("o_x", [D, TW], F32, kind="ExternalOutput").ap()
    gate_scr = nc.dram_tensor("gate_scr", [8, TW], F32, kind="Internal").ap()

    consts = alloc_consts(p)
    ones_a, ones_b, ones1, cst = consts
    NTT = TW // 512
    idb = p.f32.alloc(128)
    p.dma("sync", idb, ident)
    prm = ("dS", p.now("dS"))
    grow = p.f32.alloc(TW, parts=1)

    xb = [p.bf16.alloc(TW) for _ in range(8)]
    acc = [p.bf16.alloc(TW) for _ in range(8)]
    hid = [p.bf16.alloc(TW) for _ in range(8)]
    grep = [p.bf16.alloc(TW) for _ in range(2)]

    gt8, gtw = load_weight_slabs(p, w_gt, D, 8, dtype=F32, queue="sync")
    p.f32.mark()
    xrot = [p.f32.alloc(TW) for _ in range(2)]
    lg = p.f32.alloc(TW, parts=8)
    cdone = {}
    tmk = None
    for c in range(8):
        waits = list(cdone[c - 2]) if c - 2 in cdone else []
        w = p.dma("sync", xrot[c % 2], i_x[c * 128:(c + 1) * 128, :], waits=waits,
                  sem=f"xl{c % 2}")
        cp = p.emit("vector", lambda eng, c=c: eng.tensor_copy(xb[c], xrot[c % 2]),
                    waits=[w])
        for tt in range(NTT):
            n0 = tt * 512
            psum = p.psum[tt % 2]
            def fm(eng, c=c, n0=n0, psum=psum, first=(c == 0), last=(c == 7)):
                return eng.matmul(psum[0:8, 0:512], gt8[c], xrot[c % 2][:, n0:n0 + 512],
                                  start=first, stop=last)
            tmk = p.emit("tensor", fm, waits=[w, gtw], inc="t")
        cdone[c] = (cp, tmk)
    xb_wait = ("v", p.now("v"))
    for tt in range(NTT):
        n0 = tt * 512
        psum = p.psum[tt % 2]
        p.emit("vector", lambda eng, n0=n0, psum=psum:
               eng.tensor_copy(lg[:, n0:n0 + 512], psum[0:8, 0:512]), waits=[tmk])
    lg_wait = ("v", p.now("v"))

    lgt = p.f32.alloc(8)
    gtb = p.f32.alloc(8)
    pz = p.f32.alloc(8)
    p1 = p.f32.alloc(1)
    p2 = p.f32.alloc(1)
    rec = p.f32.alloc(1)
    gfm = p.f32.alloc(128, parts=8)
    prev = None
    gdma = None
    for blk in range(TW // 128):
        def ftr(eng, blk=blk):
            return eng.transpose(p.psum[2][0:128, 0:8], lg[:, blk * 128:(blk + 1) * 128],
                                 idb[0:8, 0:8])
        mt = p.emit("tensor", ftr, waits=[lg_wait, prm] + ([prev] if prev else []), inc="t")
        ev = p.emit("vector", lambda eng: eng.tensor_copy(lgt, p.psum[2][0:128, 0:8]),
                    waits=[mt])
        p.emit("vector", lambda eng: eng.reduce_max(p1, lgt, axis=AX.X))
        vm = p.emit("vector", lambda eng: eng.tensor_scalar_mul(p1, p1, -1.0))
        se = p.emit("scalar", lambda eng: eng.activation(lgt, lgt, AF.Exp, bias=p1,
                                                         accum_out=p2), waits=[vm])
        rc = p.emit("vector", lambda eng: eng.reciprocal(p2, p2), waits=[se])
        p.emit("vector", lambda eng: eng.tensor_scalar_mul(lgt, lgt, p2))
        p.emit("vector", lambda eng: eng.reduce_max(p1, lgt, axis=AX.X))
        p.emit("vector", lambda eng: eng.tensor_tensor(gtb, lgt, p1.broadcast_to([128, 8]),
                                                       ALU.is_equal))
        p.emit("vector", lambda eng: eng.tensor_tensor(pz, gtb, lgt, ALU.mult))
        p.emit("vector", lambda eng: eng.tensor_sub(pz, lgt, pz))
        p.emit("vector", lambda eng: eng.reduce_max(p2, pz, axis=AX.X))
        p.emit("vector", lambda eng: eng.tensor_tensor(pz, pz, p2.broadcast_to([128, 8]),
                                                       ALU.is_equal))
        p.emit("vector", lambda eng: eng.tensor_scalar_mul(gtb, gtb, p1))
        p.emit("vector", lambda eng: eng.tensor_scalar_mul(pz, pz, p2))
        p.emit("vector", lambda eng: eng.tensor_add(gtb, gtb, pz))
        p.emit("vector", lambda eng: eng.tensor_add(rec, p1, p2))
        p.emit("vector", lambda eng: eng.tensor_scalar_add(rec, rec, 1e-20))
        vr = p.emit("vector", lambda eng: eng.nop())
        p.emit("vector", lambda eng: eng.reciprocal(rec, rec), waits=[vr])
        vg = p.emit("vector", lambda eng: eng.tensor_scalar_mul(gtb, gtb, rec))
        def ftr2(eng, blk=blk):
            return eng.transpose(p.psum[3][0:8, 0:128], gtb, idb)
        mt2 = p.emit("tensor", ftr2, waits=[vg], inc="t")
        prev = p.emit("vector", lambda eng: eng.tensor_copy(gfm, p.psum[3][0:8, 0:128]),
                      waits=[mt2] + ([gdma] if gdma else []))
        gdma = p.dma("sync", gate_scr[:, blk * 128:(blk + 1) * 128], gfm, waits=[prev],
                     sem="gsc")
    gsc_done = ("gsc", p.now("gsc"))
    p.barrier()

    p.bf16.mark()
    ctx = {}
    acc_zero = []
    for c in range(8):
        acc_zero.append(p.emit("vector", lambda eng, c=c: eng.memset(acc[c], 0.0)))
    grow_busy = None
    for e in range(9):
        routed = e < 8
        if routed:
            gw = p.dma("sync", grow, gate_scr[e:e + 1, :],
                       waits=[gsc_done] + ([grow_busy] if grow_busy else []), sem="grw")
            gb = grep[e % 2]
            mk = None
            mb = None
            for tt in range(NTT):
                n0 = tt * 512
                def fb(eng, n0=n0):
                    return eng.matmul(p.psum[4][:, 0:512], ones1, grow[0:1, n0:n0 + 512],
                                      start=True, stop=True)
                mb = p.emit("tensor", fb, waits=[gw, cst] + ([mk] if mk else []), inc="t")
                mk = p.emit("vector", lambda eng, n0=n0, gb=gb:
                            eng.tensor_copy(gb[:, n0:n0 + 512], p.psum[4][:, 0:512]),
                            waits=[mb])
            grow_busy = mb
            grep_wait = mk
        wg_e = wg[e] if routed else wsg
        wu_e = wu[e] if routed else wsu
        wd_e = wd[e] if routed else wsd
        for half in range(2):
            f0 = half * 8
            def cb_g(m, tt, psum, n0, n1, wait, f0=f0):
                def fe(eng, m=m, n0=n0, n1=n1, psum=psum):
                    return eng.activation(hid[m - f0][:, n0:n1], psum[:, 0:n1 - n0],
                                          AF.Silu)
                return p.emit("scalar", fe, waits=[wait])
            gemm_cols(p, wg_e, D, xb, xb_wait, list(range(f0, f0 + 8)), cb_g,
                      group_mt=4, ctx=ctx)
            def cb_u(m, tt, psum, n0, n1, wait, f0=f0, e=e, routed=routed):
                def f1(eng, m=m, n0=n0, n1=n1, psum=psum):
                    return eng.tensor_mul(hid[m - f0][:, n0:n1], hid[m - f0][:, n0:n1],
                                          psum[:, 0:n1 - n0])
                mk1 = p.emit("vector", f1, waits=[wait])
                if routed:
                    def f2(eng, m=m, n0=n0, n1=n1, e=e, f0=f0):
                        return eng.tensor_mul(hid[m - f0][:, n0:n1], hid[m - f0][:, n0:n1],
                                              grep[e % 2][:, n0:n1])
                    mk1 = p.emit("vector", f2, waits=[grep_wait])
                return mk1
            gemm_cols(p, wu_e, D, xb, xb_wait, list(range(f0, f0 + 8)), cb_u,
                      group_mt=4, ctx=ctx)
            hid_wait = ("v", p.now("v"))
            def cb_d(m, tt, psum, n0, n1, wait):
                def fa(eng, m=m, n0=n0, n1=n1, psum=psum):
                    return eng.tensor_add(acc[m][:, n0:n1], acc[m][:, n0:n1],
                                          psum[:, 0:n1 - n0])
                return p.emit("vector", fa, waits=[wait] + acc_zero)
            gemm_cols(p, wd_e[f0 * 128:(f0 + 8) * 128, :], 1024, hid, hid_wait,
                      list(range(8)), cb_d, group_mt=4, ctx=ctx)
    acc_wait = ("v", p.now("v"))
    p.barrier()
    p.bf16.reset()
    p.f32.reset()
    emit_post_ln_stream(p, acc, i_x, o_x, consts, acc_wait, TFt=TW)
    p.drain_dmas()
    p.build()
    return p


def run_L1ffn(inp, x3):
    common = {
        "w_gt": np.ascontiguousarray(inp['L1_ffn_gate_w'], np.float32),
        "wg": np.ascontiguousarray(inp['L1_ffn_w_gate'], np.float32),
        "wu": np.ascontiguousarray(inp['L1_ffn_w_up'], np.float32),
        "wd": np.ascontiguousarray(inp['L1_ffn_w_down'], np.float32),
        "wsg": np.ascontiguousarray(inp['L1_ffn_ws_gate'], np.float32),
        "wsu": np.ascontiguousarray(inp['L1_ffn_ws_up'], np.float32),
        "wsd": np.ascontiguousarray(inp['L1_ffn_ws_down'], np.float32),
        "ident": np.eye(128, dtype=np.float32),
    }
    x4 = np.empty((B, T, D), np.float32)
    for q in range(2):
        maps = []
        for core in range(8):
            b, h = core // 2, core % 2
            t0 = h * TH + q * 1024
            m = dict(common)
            m["i_x"] = np.ascontiguousarray(x3[b].T[:, t0:t0 + 1024])
            maps.append(m)
        res = _run(build_L1ffn, maps, "L1ffn")
        for core in range(8):
            b, h = core // 2, core % 2
            t0 = h * TH + q * 1024
            x4[b, t0:t0 + 1024] = np.asarray(res[core]["o_x"], np.float32).T
    return x4


def build_L2qkv(nc):
    p = Prog(nc, f32_words=9000, bf16_words=86000)
    i_x = nc.dram_tensor("i_x", [D, TH], F32, kind="ExternalInput").ap()
    wq = nc.dram_tensor("wq", [D, D], F32, kind="ExternalInput").ap()
    wk = nc.dram_tensor("wk", [D, D], F32, kind="ExternalInput").ap()
    wv = nc.dram_tensor("wv", [D, D], F32, kind="ExternalInput").ap()
    bq = nc.dram_tensor("bq", [128, 8], F32, kind="ExternalInput").ap()
    bk = nc.dram_tensor("bk", [128, 8], F32, kind="ExternalInput").ap()
    bv = nc.dram_tensor("bv", [128, 8], F32, kind="ExternalInput").ap()
    cosT = nc.dram_tensor("cosT", [512, TH], F32, kind="ExternalInput").ap()
    sinT = nc.dram_tensor("sinT", [512, TH], F32, kind="ExternalInput").ap()
    o_q = nc.dram_tensor("o_q", [D, TH], BF16, kind="ExternalOutput").ap()
    o_k = nc.dram_tensor("o_k", [D, TH], BF16, kind="ExternalOutput").ap()
    o_v = nc.dram_tensor("o_v", [D, TH], BF16, kind="ExternalOutput").ap()

    consts = alloc_consts(p)
    ones_a, ones_b, ones1, cst = consts
    bq_sb = p.f32.alloc(8)
    bk_sb = p.f32.alloc(8)
    bv_sb = p.f32.alloc(8)
    p.dma("sync", bq_sb, bq)
    p.dma("sync", bk_sb, bk)
    p.dma("sync", bv_sb, bv)
    prm = ("dS", p.now("dS"))
    cos_sb = [p.bf16.alloc(TH) for _ in range(4)]
    sin_sb = [p.bf16.alloc(TH) for _ in range(4)]
    for c in range(4):
        p.dma("gpsimd", cos_sb[c], cosT[c * 128:(c + 1) * 128, :], sem="tbl")
        p.dma("gpsimd", sin_sb[c], sinT[c * 128:(c + 1) * 128, :], sem="tbl")
    tblw = ("tbl", p.now("tbl"))

    xb = [p.bf16.alloc(TH) for _ in range(8)]
    p.f32.mark()
    xrot = [p.f32.alloc(TH) for _ in range(2)]
    cdone = {}
    for c in range(8):
        waits = [cdone[c - 2]] if c - 2 in cdone else []
        w = p.dma("sync", xrot[c % 2], i_x[c * 128:(c + 1) * 128, :], waits=waits,
                  sem=f"xl{c % 2}")
        cdone[c] = p.emit("vector", lambda eng, c=c: eng.tensor_copy(xb[c], xrot[c % 2]),
                          waits=[w])
    xb_wait = ("v", p.now("v"))

    qf = [p.bf16.alloc(TH) for _ in range(8)]
    kf = [p.bf16.alloc(TH) for _ in range(8)]
    ctx = {}
    def mk_cb(dst, bias_sb):
        def cb(m, tt, psum, n0, n1, wait):
            def fe(eng, m=m, n0=n0, n1=n1, psum=psum):
                return eng.activation(dst[m][:, n0:n1], psum[:, 0:n1 - n0], AF.Identity,
                                      bias=bias_sb[:, m:m + 1])
            return p.emit("scalar", fe, waits=[wait, prm])
        return cb
    gemm_cols(p, wq, D, xb, xb_wait, list(range(8)), mk_cb(qf, bq_sb), group_mt=4, ctx=ctx)
    gemm_cols(p, wk, D, xb, xb_wait, list(range(8)), mk_cb(kf, bk_sb), group_mt=4, ctx=ctx)
    qk_wait = ("s", p.now("s"))
    vb_rot = [p.bf16.alloc(TH) for _ in range(2)]
    vdma = {}
    def cb_v(m, tt, psum, n0, n1, wait):
        vb = vb_rot[m % 2]
        def fe(eng, m=m, n0=n0, n1=n1, psum=psum, vb=vb):
            return eng.activation(vb[:, n0:n1], psum[:, 0:n1 - n0], AF.Identity,
                                  bias=bv_sb[:, m:m + 1])
        mk = p.emit("scalar", fe, waits=[wait, prm] + ([vdma[m - 2]] if m - 2 in vdma else []))
        if n1 == TH:
            vdma[m] = p.dma("sync", o_v[m * 128:(m + 1) * 128, :], vb, waits=[mk],
                            sem=f"vo{m % 2}")
        return mk
    gemm_cols(p, wv, D, xb, xb_wait, list(range(8)), cb_v, group_mt=4, ctx=ctx)

    rtmp = p.f32.alloc(TH)
    rtmp2 = p.f32.alloc(TH)
    obf = [p.bf16.alloc(TH) for _ in range(2)]
    qk_hbm = {0: o_q, 1: o_k}
    obf_busy = {0: None, 1: None}
    for di, dst in enumerate((qf, kf)):
        ow = qk_hbm[di]
        owr = ow.rearrange("(hh half i) t -> hh half i t", half=2, i=32)
        for c in range(4):
            c2 = c + 4
            v1 = p.emit("vector", lambda eng, c=c, dst=dst:
                        eng.tensor_mul(rtmp, dst[c], cos_sb[c]), waits=[qk_wait, tblw])
            p.emit("vector", lambda eng, c=c, dst=dst:
                   eng.tensor_mul(rtmp2, dst[c], sin_sb[c]))
            p.emit("vector", lambda eng, c=c, c2=c2, dst=dst:
                   eng.tensor_mul(dst[c], dst[c2], sin_sb[c]))
            p.emit("vector", lambda eng, c=c, dst=dst:
                   eng.tensor_sub(rtmp, rtmp, dst[c]))
            p.emit("vector", lambda eng, c=c, c2=c2, dst=dst:
                   eng.tensor_mul(dst[c2], dst[c2], cos_sb[c]))
            p.emit("vector", lambda eng, c=c, c2=c2, dst=dst:
                   eng.tensor_add(dst[c2], dst[c2], rtmp2))
            w1 = [obf_busy[0]] if obf_busy[0] else []
            m1 = p.emit("vector", lambda eng: eng.tensor_copy(obf[0], rtmp), waits=w1)
            obf_busy[0] = p.dma("sync", owr[4 * c:4 * c + 4, 0], obf[0], waits=[m1], sem="ro0")
            w2 = [obf_busy[1]] if obf_busy[1] else []
            m2 = p.emit("vector", lambda eng, c2=c2, dst=dst:
                        eng.tensor_copy(obf[1], dst[c2]), waits=w2)
            obf_busy[1] = p.dma("sync", owr[4 * c:4 * c + 4, 1], obf[1], waits=[m2], sem="ro1")
    p.drain_dmas()
    p.build()
    return p


def build_attn_core(nc, causal):
    p = Prog(nc, f32_words=14000, bf16_words=62000)
    i_q = nc.dram_tensor("i_q", [512, T], BF16, kind="ExternalInput").ap()
    i_k = nc.dram_tensor("i_k", [512, T], BF16, kind="ExternalInput").ap()
    i_v = nc.dram_tensor("i_v", [512, T], BF16, kind="ExternalInput").ap()
    i_masks = nc.dram_tensor("i_masks", [128, 4 * 512], BF16, kind="ExternalInput").ap()
    i_idb = nc.dram_tensor("i_idb", [128, 128], BF16, kind="ExternalInput").ap()
    i_idf = nc.dram_tensor("i_idf", [128, 128], F32, kind="ExternalInput").ap()
    o_out = nc.dram_tensor("o_out", [512, T], BF16, kind="ExternalOutput").ap()

    qsb = [p.bf16.alloc(T) for _ in range(4)]
    ksb = [p.bf16.alloc(T) for _ in range(4)]
    vsb = [p.bf16.alloc(T) for _ in range(4)]
    for c in range(4):
        p.dma("gpsimd", qsb[c], i_q[c * 128:(c + 1) * 128, :], sem="qkv")
        p.dma("gpsimd", ksb[c], i_k[c * 128:(c + 1) * 128, :], sem="qkv")
        p.dma("gpsimd", vsb[c], i_v[c * 128:(c + 1) * 128, :], sem="qkv")
    masks = p.bf16.alloc(4 * 512)
    idb = p.bf16.alloc(128)
    idf = p.f32.alloc(128)
    ones1b = p.bf16.alloc(512, parts=1)
    ones1f = p.f32.alloc(128, parts=1)
    onecolb = p.bf16.alloc(1)
    onecolf = p.f32.alloc(1)
    p.dma("sync", masks, i_masks)
    p.dma("sync", idb, i_idb)
    p.dma("sync", idf, i_idf)
    ld = ("dS", p.now("dS"))
    qkvw = ("qkv", p.now("qkv"))
    p.emit("vector", lambda eng: eng.memset(ones1b, 1.0))
    p.emit("vector", lambda eng: eng.memset(ones1f, 1.0))
    p.emit("vector", lambda eng: eng.memset(onecolb, 1.0))
    p.emit("vector", lambda eng: eng.memset(onecolf, 1.0))
    cst = p.emit("vector", lambda eng: eng.nop())

    nq = p.f32.alloc(T, parts=1)
    nk = p.f32.alloc(T, parts=1)
    rmax = p.f32.alloc(T, parts=1)
    negs = p.bf16.alloc(T, parts=1)
    sq = p.f32.alloc(512, parts=64)
    vt = p.bf16.alloc(T // 2)
    vcast = p.f32.alloc(128, parts=64)
    lrec = p.f32.alloc(512, parts=1)
    psb = [p.bf16.alloc(512) for _ in range(2)]
    obuf = [p.bf16.alloc(512) for _ in range(2)]

    NKT = T // 128
    NQT = T // 512
    ps_st = [p.psum[0], p.psum[1]]
    ps_l = p.psum[2]
    ps_o = p.psum[3]
    ps_rr = p.psum[4]
    ps_tr = p.psum[5]
    ps_n = p.psum[6]

    pv = {}
    for u in range(8):
        p.barrier()
        qb = (u % 2) * 64
        qu = qsb[u // 2][qb:qb + 64, :]
        ku = ksb[u // 2][qb:qb + 64, :]
        vu = vsb[u // 2][qb:qb + 64, :]
        for src_ap, row in ((qu, nq), (ku, nk)):
            for tt in range(NQT):
                n0 = tt * 512
                s1 = p.emit("scalar", lambda eng, src_ap=src_ap, n0=n0:
                            eng.activation(sq, src_ap[:, n0:n0 + 512], AF.Square),
                            waits=[qkvw, pv.get("nmm")])
                mm = p.emit("tensor", lambda eng: eng.matmul(ps_n[0:1, 0:512],
                                                             onecolf[0:64, 0:1], sq,
                                                             start=True, stop=True),
                            waits=[s1, cst, pv.get("nev")], inc="t")
                pv["nmm"] = mm
                pv["nev"] = p.emit("vector", lambda eng, row=row, n0=n0:
                                   eng.tensor_copy(row[0:1, n0:n0 + 512],
                                                   ps_n[0:1, 0:512]), waits=[mm])
        s2 = p.emit("scalar", lambda eng: eng.activation(nq, nq, AF.Sqrt),
                    waits=[pv["nev"]])
        s3 = p.emit("scalar", lambda eng: eng.activation(nk, nk, AF.Sqrt))
        if causal:
            rx = p.emit("vector", lambda eng: eng.tensor_tensor_scan(
                rmax, nk, nk, 0.0, ALU.max, ALU.max), waits=[s3])
            bd = p.emit("vector", lambda eng: eng.tensor_mul(rmax, rmax, nq))
        else:
            r0 = p.emit("vector", lambda eng: eng.reduce_max(lrec[0:1, 0:1], nk, axis=AX.X),
                        waits=[s3])
            bd = p.emit("vector", lambda eng: eng.tensor_scalar(
                rmax, nq, lrec[0:1, 0:1], None, ALU.mult, ALU.bypass), waits=[r0])
        p.emit("vector", lambda eng: eng.tensor_scalar_min(rmax, rmax, 640.0))
        ns = p.emit("vector", lambda eng: eng.tensor_scalar_mul(negs, rmax, -1.0))
        for kt in range(NKT):
            cv = p.emit("vector", lambda eng, kt=kt, vu=vu:
                        eng.tensor_copy(vcast, vu[:, kt * 128:(kt + 1) * 128]),
                        waits=[qkvw, pv.get("vtm")])
            tv = p.emit("tensor", lambda eng: eng.transpose(ps_tr[0:128, 0:64], vcast, idf[0:64, 0:64]),
                        waits=[cv, ld, pv.get("vte")], inc="t")
            pv["vtm"] = tv
            pv["vte"] = p.emit("vector", lambda eng, kt=kt:
                               eng.tensor_copy(vt[:, kt * 64:(kt + 1) * 64],
                                               ps_tr[0:128, 0:64]), waits=[tv])
        vt_wait = pv["vte"]
        for qt in range(NQT):
            n0 = qt * 512
            nkt = min(NKT, 4 * qt + 4) if causal else NKT
            for kt in range(nkt):
                bank = ps_st[kt % 2]
                w0 = [qkvw, ns]
                if ("exp", kt % 2) in pv:
                    w0.append(pv.pop(("exp", kt % 2)))
                m1 = p.emit("tensor", lambda eng, kt=kt, n0=n0, qu=qu, ku=ku, bank=bank:
                            eng.matmul(bank[:, 0:512], ku[:, kt * 128:(kt + 1) * 128],
                                       qu[:, n0:n0 + 512], start=True, stop=False),
                            waits=w0)
                if causal and kt >= 4 * qt:
                    di = kt - 4 * qt
                    p.emit("tensor", lambda eng, di=di, bank=bank:
                           eng.matmul(bank[:, 0:512], idb,
                                      masks[:, di * 512:(di + 1) * 512],
                                      start=False, stop=False), waits=[ld])
                m2 = p.emit("tensor", lambda eng, n0=n0, bank=bank:
                            eng.matmul(bank[:, 0:512], ones1b[0:1, 0:128],
                                       negs[0:1, n0:n0 + 512], start=False, stop=True),
                            inc="t")
                pb = psb[kt % 2]
                ex = p.emit("scalar", lambda eng, bank=bank, pb=pb:
                            eng.activation(pb, bank[:, 0:512], AF.Exp, scale=0.125),
                            waits=[m2, pv.get(("pvm", kt % 2))])
                pv[("exp", kt % 2)] = ex
                ml = p.emit("tensor", lambda eng, pb=pb, first=(kt == 0), last=(kt == nkt - 1):
                            eng.matmul(ps_l[0:1, 0:512], onecolb[0:128, 0:1], pb,
                                       start=first, stop=last),
                            waits=[ex, cst] + ([pv.pop("lev")] if (kt == 0 and "lev" in pv) else []),
                            inc="t" if kt == nkt - 1 else None)
                mo = p.emit("tensor", lambda eng, kt=kt, pb=pb, first=(kt == 0),
                            last=(kt == nkt - 1):
                            eng.matmul(ps_o[0:64, 0:512], vt[:, kt * 64:(kt + 1) * 64], pb,
                                       start=first, stop=last),
                            waits=[vt_wait] + ([pv.pop("oev")] if (kt == 0 and "oev" in pv) else []),
                            inc="t" if kt == nkt - 1 else None)
                pv[("pvm", kt % 2)] = mo
            le = p.emit("vector", lambda eng: eng.tensor_copy(lrec, ps_l[0:1, 0:512]),
                        waits=[ml])
            pv["lev"] = le
            rc = p.emit("vector", lambda eng: eng.reciprocal(lrec, lrec))
            mr = p.emit("tensor", lambda eng: eng.matmul(ps_rr[0:64, 0:512],
                                                         ones1f[0:1, 0:64], lrec,
                                                         start=True, stop=True),
                        waits=[rc, pv.get("rrev")], inc="t")
            ob = obuf[qt % 2]
            oe = p.emit("vector", lambda eng, ob=ob: eng.tensor_copy(ob[0:64, :],
                                                                      ps_o[0:64, 0:512]),
                        waits=[mo, pv.get(("odma", qt % 2))])
            pv["oev"] = oe
            pv["rrev"] = p.emit("vector", lambda eng, ob=ob:
                                eng.tensor_mul(ob[0:64, :], ob[0:64, :],
                                               ps_rr[0:64, 0:512]), waits=[mr])
            pv[("odma", qt % 2)] = p.dma("sync", o_out[u * 64:(u + 1) * 64, n0:n0 + 512],
                                         ob[0:64, :], waits=[pv["rrev"]], sem=f"oo{qt % 2}")
    p.drain_dmas()
    p.build()
    return p


def build_L2out(nc):
    p = Prog(nc, f32_words=33000, bf16_words=38000)
    i_ao = nc.dram_tensor("i_ao", [D, TH], BF16, kind="ExternalInput").ap()
    i_x = nc.dram_tensor("i_x", [D, TH], F32, kind="ExternalInput").ap()
    w_o = nc.dram_tensor("w_o", [D, D], F32, kind="ExternalInput").ap()
    b_o = nc.dram_tensor("b_o", [128, 8], F32, kind="ExternalInput").ap()
    o_x = nc.dram_tensor("o_x", [D, TH], F32, kind="ExternalOutput").ap()

    consts = alloc_consts(p)
    ones_a, ones_b, ones1, cst = consts
    bo_sb = p.f32.alloc(8)
    bw = p.dma("sync", bo_sb, b_o)
    ab = [p.bf16.alloc(TH) for _ in range(8)]
    aw = None
    for k in range(8):
        aw = p.dma("gpsimd", ab[k], i_ao[k * 128:(k + 1) * 128, :], sem="al")
    acc = [p.f32.alloc(TH) for _ in range(8)]
    slabs, ww = load_weight_slabs(p, w_o, D, D)

    def out_cb(m, tt, psum, n0, n1, wait):
        def fe(eng, m=m, n0=n0, n1=n1, psum=psum):
            return eng.activation(acc[m][:, n0:n1], psum[:, 0:n1 - n0], AF.Identity,
                                  bias=bo_sb[:, m:m + 1])
        return p.emit("scalar", fe, waits=[wait, bw])
    gemm_fm(p, slabs, ww, ab, 8, out_cb, rhs_wait=aw)
    acc_wait = ("s", p.now("s"))
    p.barrier()
    emit_post_ln_stream(p, acc, i_x, o_x, consts, acc_wait)
    p.drain_dmas()
    p.build()
    return p


def _l2_perm():
    perm = np.zeros(D, np.int64)
    for h in range(16):
        for i in range(32):
            perm[h * 32 + i] = h * 64 + 2 * i
            perm[512 + h * 32 + i] = h * 64 + 2 * i + 1
    return perm


def _causal_masks():
    m = np.zeros((128, 4 * 512), np.float32)
    j = np.arange(512)
    for di in range(4):
        for pp in range(128):
            m[pp, di * 512:(di + 1) * 512] = np.where(j >= 128 * di + pp, 0.0, -1e30)
    return m.astype(BF)


def run_L2attn(inp, x4):
    perm = _l2_perm()
    pos = np.arange(T, dtype=np.float32)
    freq = np.arange(0, 64, 2, dtype=np.float32) / 64
    inv = (1.0 / 10000.0 ** freq).astype(np.float32)
    ang = np.outer(pos, inv)
    cos32 = np.cos(ang).T.astype(np.float32)
    sin32 = np.sin(ang).T.astype(np.float32)
    cosT = np.tile(cos32, (16, 1))
    sinT = np.tile(sin32, (16, 1))
    wq = np.ascontiguousarray(np.asarray(inp['L2_attn_Wq'], np.float32)[:, perm])
    wk = np.ascontiguousarray(np.asarray(inp['L2_attn_Wk'], np.float32)[:, perm])
    bqp = np.asarray(inp['L2_attn_bq'], np.float32)[perm]
    bkp = np.asarray(inp['L2_attn_bk'], np.float32)[perm]
    common = {"wq": wq, "wk": wk,
              "wv": np.ascontiguousarray(inp['L2_attn_Wv'], np.float32),
              "bq": pack_pp(bqp, 8), "bk": pack_pp(bkp, 8),
              "bv": pack_pp(inp['L2_attn_bv'], 8)}
    maps = []
    for core in range(8):
        b, h = core // 2, core % 2
        t0 = h * TH
        m = dict(common)
        m["i_x"] = np.ascontiguousarray(x4[b].T[:, t0:t0 + TH])
        m["cosT"] = np.ascontiguousarray(cosT[:, t0:t0 + TH])
        m["sinT"] = np.ascontiguousarray(sinT[:, t0:t0 + TH])
        maps.append(m)
    r1 = _run(build_L2qkv, maps, "L2qkv")
    Q = {}; Kk = {}; V = {}
    for b in range(B):
        q = np.concatenate([np.asarray(r1[2 * b]["o_q"]), np.asarray(r1[2 * b + 1]["o_q"])], axis=1)
        k = np.concatenate([np.asarray(r1[2 * b]["o_k"]), np.asarray(r1[2 * b + 1]["o_k"])], axis=1)
        v = np.concatenate([np.asarray(r1[2 * b]["o_v"]), np.asarray(r1[2 * b + 1]["o_v"])], axis=1)
        Q[b], Kk[b], V[b] = q, k, v
    masks = _causal_masks()
    idb = np.eye(128).astype(BF)
    idf = np.eye(128, dtype=np.float32)
    maps2 = []
    for core in range(8):
        b, g = core // 2, core % 2
        rows = slice(g * 512, (g + 1) * 512)
        maps2.append({"i_q": np.ascontiguousarray(Q[b][rows]),
                      "i_k": np.ascontiguousarray(Kk[b][rows]),
                      "i_v": np.ascontiguousarray(V[b][rows]),
                      "i_masks": masks, "i_idb": idb, "i_idf": idf})
    r2 = _run(lambda nc: build_attn_core(nc, True), maps2, "attn_causal")
    maps3 = []
    for core in range(8):
        b, h = core // 2, core % 2
        t0 = h * TH
        ao = np.concatenate([np.asarray(r2[2 * b]["o_out"]),
                             np.asarray(r2[2 * b + 1]["o_out"])], axis=0)
        maps3.append({"i_ao": np.ascontiguousarray(ao[:, t0:t0 + TH]),
                      "i_x": np.ascontiguousarray(x4[b].T[:, t0:t0 + TH]),
                      "w_o": np.ascontiguousarray(inp['L2_attn_Wo'], np.float32),
                      "b_o": pack_pp(inp['L2_attn_bo'], 8)})
    r3 = _run(build_L2out, maps3, "L2out")
    return _unshard_fm([np.asarray(r["o_x"], np.float32) for r in r3])


def emit_top2_gates(p, lg, TW, idb_ap, gate_scr, eps, prm):
    lgt = p.f32.alloc(8)
    gtb = p.f32.alloc(8)
    pz = p.f32.alloc(8)
    p1 = p.f32.alloc(1)
    p2 = p.f32.alloc(1)
    rec = p.f32.alloc(1)
    gfm = p.f32.alloc(128, parts=8)
    prev = None
    gdma = None
    lg_wait = ("v", p.now("v"))
    for blk in range(TW // 128):
        def ftr(eng, blk=blk):
            return eng.transpose(p.psum[2][0:128, 0:8], lg[:, blk * 128:(blk + 1) * 128],
                                 idb_ap[0:8, 0:8])
        mt = p.emit("tensor", ftr, waits=[lg_wait, prm] + ([prev] if prev else []), inc="t")
        ev = p.emit("vector", lambda eng: eng.tensor_copy(lgt, p.psum[2][0:128, 0:8]),
                    waits=[mt])
        p.emit("vector", lambda eng: eng.reduce_max(p1, lgt, axis=AX.X))
        vm = p.emit("vector", lambda eng: eng.tensor_scalar_mul(p1, p1, -1.0))
        se = p.emit("scalar", lambda eng: eng.activation(lgt, lgt, AF.Exp, bias=p1,
                                                         accum_out=p2), waits=[vm])
        rc = p.emit("vector", lambda eng: eng.reciprocal(p2, p2), waits=[se])
        p.emit("vector", lambda eng: eng.tensor_scalar_mul(lgt, lgt, p2))
        p.emit("vector", lambda eng: eng.reduce_max(p1, lgt, axis=AX.X))
        p.emit("vector", lambda eng: eng.tensor_tensor(gtb, lgt, p1.broadcast_to([128, 8]),
                                                       ALU.is_equal))
        p.emit("vector", lambda eng: eng.tensor_tensor(pz, gtb, lgt, ALU.mult))
        p.emit("vector", lambda eng: eng.tensor_sub(pz, lgt, pz))
        p.emit("vector", lambda eng: eng.reduce_max(p2, pz, axis=AX.X))
        p.emit("vector", lambda eng: eng.tensor_tensor(pz, pz, p2.broadcast_to([128, 8]),
                                                       ALU.is_equal))
        p.emit("vector", lambda eng: eng.tensor_scalar_mul(gtb, gtb, p1))
        p.emit("vector", lambda eng: eng.tensor_scalar_mul(pz, pz, p2))
        p.emit("vector", lambda eng: eng.tensor_add(gtb, gtb, pz))
        p.emit("vector", lambda eng: eng.tensor_add(rec, p1, p2))
        if eps:
            p.emit("vector", lambda eng: eng.tensor_scalar_add(rec, rec, eps))
        vr = p.emit("vector", lambda eng: eng.nop())
        p.emit("vector", lambda eng: eng.reciprocal(rec, rec), waits=[vr])
        vg = p.emit("vector", lambda eng: eng.tensor_scalar_mul(gtb, gtb, rec))
        def ftr2(eng, blk=blk):
            return eng.transpose(p.psum[3][0:8, 0:128], gtb, idb_ap)
        mt2 = p.emit("tensor", ftr2, waits=[vg], inc="t")
        prev = p.emit("vector", lambda eng: eng.tensor_copy(gfm, p.psum[3][0:8, 0:128]),
                      waits=[mt2] + ([gdma] if gdma else []))
        gdma = p.dma("sync", gate_scr[:, blk * 128:(blk + 1) * 128], gfm, waits=[prev],
                     sem="gsc")
    return ("gsc", p.now("gsc"))


def build_L2ffn(nc):
    TW = 1024
    p = Prog(nc, f32_words=22500, bf16_words=56600)
    i_x = nc.dram_tensor("i_x", [D, TW], F32, kind="ExternalInput").ap()
    encW = nc.dram_tensor("encW", [D, 2048], F32, kind="ExternalInput").ap()
    encB = nc.dram_tensor("encB", [128, 16], F32, kind="ExternalInput").ap()
    s2cW = nc.dram_tensor("s2cW", [2048, 64], F32, kind="ExternalInput").ap()
    s2cB = nc.dram_tensor("s2cB", [64, 1], F32, kind="ExternalInput").ap()
    rW1 = nc.dram_tensor("rW1", [64, 64], F32, kind="ExternalInput").ap()
    rb1 = nc.dram_tensor("rb1", [64, 1], F32, kind="ExternalInput").ap()
    rW2 = nc.dram_tensor("rW2", [64, 8], F32, kind="ExternalInput").ap()
    rb2 = nc.dram_tensor("rb2", [8, 1], F32, kind="ExternalInput").ap()
    We1 = nc.dram_tensor("We1", [8, 64, 1024], F32, kind="ExternalInput").ap()
    be1 = nc.dram_tensor("be1", [128, 8 * 8], F32, kind="ExternalInput").ap()
    We2 = nc.dram_tensor("We2", [8, 1024, 64], F32, kind="ExternalInput").ap()
    be2 = nc.dram_tensor("be2", [8, 64], F32, kind="ExternalInput").ap()
    c2sW = nc.dram_tensor("c2sW", [64, 2048], F32, kind="ExternalInput").ap()
    c2sB = nc.dram_tensor("c2sB", [128, 16], F32, kind="ExternalInput").ap()
    decW = nc.dram_tensor("decW", [2048, D], F32, kind="ExternalInput").ap()
    decB = nc.dram_tensor("decB", [128, 8], F32, kind="ExternalInput").ap()
    lnG = nc.dram_tensor("lnG", [128, 8], F32, kind="ExternalInput").ap()
    lnB = nc.dram_tensor("lnB", [128, 8], F32, kind="ExternalInput").ap()
    ident = nc.dram_tensor("ident", [128, 128], F32, kind="ExternalInput").ap()
    o_x = nc.dram_tensor("o_x", [D, TW], F32, kind="ExternalOutput").ap()
    gate_scr = nc.dram_tensor("gate_scr", [8, TW], F32, kind="Internal").ap()

    consts = alloc_consts(p)
    ones_a, ones_b, ones1, cst = consts
    NTT = TW // 512
    encB_sb = p.f32.alloc(16)
    s2cB_sb = p.f32.alloc(1)
    rb1_sb = p.f32.alloc(1)
    rb2_sb = p.f32.alloc(1)
    be1_sb = p.f32.alloc(64)
    c2sB_sb = p.f32.alloc(16)
    decB_sb = p.f32.alloc(8)
    lnG_sb = p.f32.alloc(8)
    lnB_sb = p.f32.alloc(8)
    idb = p.f32.alloc(128)
    for dst, src_ in ((encB_sb, encB), (s2cB_sb, s2cB), (rb1_sb, rb1), (rb2_sb, rb2),
                      (be1_sb, be1), (c2sB_sb, c2sB), (decB_sb, decB), (lnG_sb, lnG),
                      (lnB_sb, lnB), (idb, ident)):
        p.dma("sync", dst[0:src_.shape[0], 0:src_.shape[1]], src_)
    be2_sb = p.bf16.alloc(64, parts=8)
    p.dma("gpsimd", be2_sb, be2, sem="b2l")
    b2w = ("b2l", p.now("b2l"))
    prm = ("dS", p.now("dS"))

    xb = [p.bf16.alloc(TW) for _ in range(8)]
    p.f32.mark()
    xrot = [p.f32.alloc(TW) for _ in range(2)]
    cdone = {}
    for c in range(8):
        waits = [cdone[c - 2]] if c - 2 in cdone else []
        w = p.dma("sync", xrot[c % 2], i_x[c * 128:(c + 1) * 128, :], waits=waits,
                  sem=f"xl{c % 2}")
        cdone[c] = p.emit("vector", lambda eng, c=c: eng.tensor_copy(xb[c], xrot[c % 2]),
                          waits=[w])
    xb_wait = ("v", p.now("v"))

    sp = [p.bf16.alloc(TW) for _ in range(16)]
    ctx = {}
    def cb_sp(m, tt, psum, n0, n1, wait):
        def fe(eng, m=m, n0=n0, n1=n1, psum=psum):
            return eng.activation(sp[m][:, n0:n1], psum[:, 0:n1 - n0], AF.Sigmoid,
                                  bias=encB_sb[:, m:m + 1])
        return p.emit("scalar", fe, waits=[wait, prm])
    gemm_cols(p, encW, D, xb, xb_wait, list(range(16)), cb_sp, group_mt=4, ctx=ctx)
    sp_wait = ("s", p.now("s"))
    s2c_slabs, s2c_w = load_weight_slabs(p, s2cW, 2048, 64)
    cont = p.bf16.alloc(TW, parts=64)
    eb = {}
    for tt in range(NTT):
        n0 = tt * 512
        psum = p.psum[tt % 2]
        for k in range(16):
            def fm(eng, k=k, n0=n0, psum=psum, first=(k == 0), last=(k == 15)):
                return eng.matmul(psum[0:64, 0:512], s2c_slabs[k], sp[k][:, n0:n0 + 512],
                                  start=first, stop=last)
            waits = ([s2c_w, sp_wait] + ([eb[tt % 2]] if tt % 2 in eb else [])) if k == 0 else []
            tmk = p.emit("tensor", fm, waits=waits, inc="t" if k == 15 else None)
        eb[tt % 2] = p.emit("scalar", lambda eng, n0=n0, psum=psum:
                            eng.activation(cont[:, n0:n0 + 512], psum[0:64, 0:512],
                                           AF.Identity, bias=s2cB_sb[0:64, 0:1]),
                            waits=[tmk, prm])
    cont_wait = ("s", p.now("s"))
    rw1s, rw1w = load_weight_slabs(p, rW1, 64, 64)
    rw2s, rw2w = load_weight_slabs(p, rW2, 64, 8)
    th = p.bf16.alloc(TW, parts=64)
    lg8 = p.f32.alloc(TW, parts=8)
    pb = {}
    for tt in range(NTT):
        n0 = tt * 512
        def fm1(eng, n0=n0):
            return eng.matmul(p.psum[4][0:64, 0:512], rw1s[0], cont[:, n0:n0 + 512],
                              start=True, stop=True)
        m1 = p.emit("tensor", fm1, waits=[rw1w, cont_wait] + ([pb.get(4)] if pb.get(4) else []), inc="t")
        pb[4] = p.emit("scalar", lambda eng, n0=n0: eng.activation(th[:, n0:n0 + 512],
                                                                   p.psum[4][0:64, 0:512],
                                                                   AF.Tanh, bias=rb1_sb[0:64, 0:1]),
                       waits=[m1])
        def fm2(eng, n0=n0):
            return eng.matmul(p.psum[5][0:8, 0:512], rw2s[0], th[:, n0:n0 + 512],
                              start=True, stop=True)
        m2 = p.emit("tensor", fm2, waits=[rw2w, pb[4]] + ([pb.get(5)] if pb.get(5) else []), inc="t")
        pb[5] = p.emit("scalar", lambda eng, n0=n0: eng.activation(lg8[:, n0:n0 + 512],
                                                                   p.psum[5][0:8, 0:512],
                                                                   AF.Identity, bias=rb2_sb[0:8, 0:1]),
                       waits=[m2])
    p.emit("vector", None, waits=[pb[5]])
    gsc_done = emit_top2_gates(p, lg8, TW, idb, gate_scr, 0.0, prm)
    p.barrier()
    p.f32.reset()

    grow = p.f32.alloc(TW, parts=1)
    gmat = p.bf16.alloc(TW, parts=8)
    of = p.bf16.alloc(TW, parts=64)
    gm_w = p.dma("gpsimd", gmat, gate_scr, waits=[gsc_done], sem="gmr")
    p.bf16.mark()
    hid = [p.bf16.alloc(TW) for _ in range(8)]
    grep = [p.bf16.alloc(TW) for _ in range(2)]
    ofp = [p.psum[0], p.psum[1]]
    grow_busy = None
    e1s = {}
    e2s = {}
    for e in range(8):
        gw = p.dma("sync", grow, gate_scr[e:e + 1, :],
                   waits=[gsc_done] + ([grow_busy] if grow_busy else []), sem="grw")
        gb = grep[e % 2]
        mk = None
        mb = None
        for tt in range(NTT):
            n0 = tt * 512
            def fb(eng, n0=n0):
                return eng.matmul(p.psum[4][:, 0:512], ones1, grow[0:1, n0:n0 + 512],
                                  start=True, stop=True)
            mb = p.emit("tensor", fb, waits=[gw, cst] + ([mk] if mk else []), inc="t")
            mk = p.emit("vector", lambda eng, n0=n0, gb=gb:
                        eng.tensor_copy(gb[:, n0:n0 + 512], p.psum[4][:, 0:512]),
                        waits=[mb])
        grow_busy = mb
        grep_wait = mk
        p.bf16.mark()
        we1s, we1w = load_weight_slabs(p, We1[e], 64, 1024, gate=e2s.get("fo"))
        we2s, we2w = load_weight_slabs(p, We2[e], 1024, 64)
        for m in range(8):
            for tt in range(NTT):
                n0 = tt * 512
                psum = p.psum[(m * NTT + tt) % 2 + 2]
                ky = (m * NTT + tt) % 2 + 2
                def fm(eng, m=m, n0=n0, psum=psum):
                    return eng.matmul(psum[:, 0:512], we1s[0][:, m * 128:(m + 1) * 128],
                                      cont[:, n0:n0 + 512], start=True, stop=True)
                mm = p.emit("tensor", fm, waits=[we1w, cont_wait] + ([e1s[ky]] if ky in e1s else []), inc="t")
                s1 = p.emit("scalar", lambda eng, m=m, e=e, n0=n0, psum=psum:
                            eng.activation(hid[m][:, n0:n0 + 512], psum[:, 0:512],
                                           AF.Relu, bias=be1_sb[:, e * 8 + m:e * 8 + m + 1]),
                            waits=[mm, prm, e2s.get("fo")])
                e1s[ky] = p.emit("vector", lambda eng, m=m, n0=n0, e=e:
                                 eng.tensor_mul(hid[m][:, n0:n0 + 512], hid[m][:, n0:n0 + 512],
                                                grep[e % 2][:, n0:n0 + 512]),
                                 waits=[s1, grep_wait])
        hid_wait = ("v", p.now("v"))
        fo_mark = None
        for tt in range(NTT):
            n0 = tt * 512
            for k in range(8):
                def fo(eng, k=k, n0=n0, tt=tt, e=e):
                    return eng.matmul(ofp[tt][0:64, 0:512], we2s[k], hid[k][:, n0:n0 + 512],
                                      start=(e == 0 and k == 0), stop=False)
                fo_mark = p.emit("tensor", fo, waits=[we2w, hid_wait],
                                 inc="t" if (tt == NTT - 1 and k == 7) else None)
        e2s["fo"] = fo_mark
        p.bf16.reset()
    ofm = {}
    for tt in range(NTT):
        n0 = tt * 512
        def fb2(eng, tt=tt, n0=n0):
            return eng.matmul(ofp[tt][0:64, 0:512], be2_sb, gmat[:, n0:n0 + 512],
                              start=False, stop=True)
        ofm[tt] = p.emit("tensor", fb2, waits=[b2w, gm_w], inc="t")
    of_ev = {}
    for tt in range(NTT):
        n0 = tt * 512
        of_ev[tt] = p.emit("vector", lambda eng, tt=tt, n0=n0:
                           eng.tensor_copy(of[:, n0:n0 + 512], ofp[tt][0:64, 0:512]),
                           waits=[ofm[tt]])
    of_wait = ("v", p.now("v"))
    p.barrier()
    p.bf16.reset()

    c2ss, c2sw = load_weight_slabs(p, c2sW, 64, 2048)
    smb = {}
    for m in range(16):
        for tt in range(NTT):
            n0 = tt * 512
            psum = p.psum[(m * NTT + tt) % 4 + 2]
            ky = (m * NTT + tt) % 4 + 2
            def fm(eng, m=m, n0=n0, psum=psum):
                return eng.matmul(psum[:, 0:512], c2ss[0][:, m * 128:(m + 1) * 128],
                                  of[:, n0:n0 + 512], start=True, stop=True)
            mm = p.emit("tensor", fm, waits=[c2sw, of_wait] + ([smb[ky]] if ky in smb else []), inc="t")
            smb[ky] = p.emit("scalar", lambda eng, m=m, n0=n0, psum=psum:
                             eng.activation(sp[m][:, n0:n0 + 512], psum[:, 0:512],
                                            AF.Sigmoid, bias=c2sB_sb[:, m:m + 1]),
                             waits=[mm, prm])
    sm_wait = ("s", p.now("s"))
    dec = [p.f32.alloc(TW) for _ in range(8)]
    decs, decw = load_weight_slabs(p, decW, 2048, D)
    def cb_dec(m, tt, psum, n0, n1, wait):
        def fe(eng, m=m, n0=n0, n1=n1, psum=psum):
            return eng.activation(dec[m][:, n0:n1], psum[:, 0:n1 - n0], AF.Sigmoid,
                                  bias=decB_sb[:, m:m + 1])
        return p.emit("scalar", fe, waits=[wait, prm])
    gemm_fm(p, decs, decw, sp, 8, cb_dec, rhs_wait=sm_wait)
    dec_wait = ("s", p.now("s"))
    p.barrier()
    mu, rstd, ready = ln_stats_resident(p, dec, TW, ones_a, ones_b, cst,
                                        ready_waits=[dec_wait])
    ln_apply_resident(p, dec, mu, rstd, ready, ones1, cst, TW, lnw=lnG_sb, lnb=lnB_sb)
    acc_wait = ("v", p.now("v"))
    p.barrier()
    emit_post_ln_stream(p, dec, i_x, o_x, consts, acc_wait, TFt=TW)
    p.drain_dmas()
    p.build()
    return p


def run_L2ffn(inp, x5):
    be1 = np.asarray(inp['L2_ffn_be1'], np.float32)
    be1p = np.zeros((128, 64), np.float32)
    for e in range(8):
        for m in range(8):
            be1p[:, e * 8 + m] = be1[e, m * 128:(m + 1) * 128]
    common = {
        "encW": np.ascontiguousarray(inp['L2_ffn_enc_W'], np.float32),
        "encB": pack_pp(inp['L2_ffn_enc_b'], 16),
        "s2cW": np.ascontiguousarray(inp['L2_ffn_s2c_W'], np.float32),
        "s2cB": np.asarray(inp['L2_ffn_s2c_b'], np.float32).reshape(64, 1),
        "rW1": np.ascontiguousarray(inp['L2_ffn_rW1'], np.float32),
        "rb1": np.asarray(inp['L2_ffn_rb1'], np.float32).reshape(64, 1),
        "rW2": np.ascontiguousarray(inp['L2_ffn_rW2'], np.float32),
        "rb2": np.asarray(inp['L2_ffn_rb2'], np.float32).reshape(8, 1),
        "We1": np.ascontiguousarray(inp['L2_ffn_We1'], np.float32),
        "be1": be1p,
        "We2": np.ascontiguousarray(inp['L2_ffn_We2'], np.float32),
        "be2": np.ascontiguousarray(inp['L2_ffn_be2'], np.float32),
        "c2sW": np.ascontiguousarray(inp['L2_ffn_c2s_W'], np.float32),
        "c2sB": pack_pp(inp['L2_ffn_c2s_b'], 16),
        "decW": np.ascontiguousarray(inp['L2_ffn_dec_W'], np.float32),
        "decB": pack_pp(inp['L2_ffn_dec_b'], 8),
        "lnG": pack_pp(inp['L2_ffn_ln_g'], 8),
        "lnB": pack_pp(inp['L2_ffn_ln_b'], 8),
        "ident": np.eye(128, dtype=np.float32),
    }
    x6 = np.empty((B, T, D), np.float32)
    for q in range(2):
        maps = []
        for core in range(8):
            b, h = core // 2, core % 2
            t0 = h * TH + q * 1024
            m = dict(common)
            m["i_x"] = np.ascontiguousarray(x5[b].T[:, t0:t0 + 1024])
            maps.append(m)
        res = _run(build_L2ffn, maps, "L2ffn")
        for core in range(8):
            b, h = core // 2, core % 2
            t0 = h * TH + q * 1024
            x6[b, t0:t0 + 1024] = np.asarray(res[core]["o_x"], np.float32).T
    return x6


def _cw_consts():
    import struct
    def trunc14(x):
        u = struct.unpack('>I', struct.pack('>f', np.float32(x)))[0]
        u &= 0xFFFFC000 | 0x00000000
        u &= ~0x3FF
        return struct.unpack('>f', struct.pack('>I', u))[0]
    twopi = 2 * np.pi
    PA = np.float64(512 * twopi)
    CA1 = trunc14(PA)
    CA2 = trunc14(PA - CA1)
    CA3 = float(np.float32(PA - CA1 - np.float64(CA2)))
    CB1 = trunc14(twopi)
    CB2 = trunc14(twopi - CB1)
    CB3 = float(np.float32(twopi - CB1 - np.float64(CB2)))
    return (float(1.0 / PA), float(CA1), float(CA2), CA3,
            float(1.0 / twopi), float(CB1), float(CB2), CB3)


def emit_mod2pi(p, th, tmpk, tmpr, waits):
    iA, CA1, CA2, CA3, iB, CB1, CB2, CB3 = _cw_consts()
    MAGIC = 12582912.0
    p.emit("vector", lambda eng: eng.tensor_scalar(tmpk, th, iA, MAGIC, ALU.mult, ALU.add),
           waits=waits)
    p.emit("vector", lambda eng: eng.tensor_scalar_add(tmpk, tmpk, -MAGIC))
    for C in (CA1, CA2, CA3):
        p.emit("vector", lambda eng, C=C: eng.scalar_tensor_tensor(th, tmpk, -C, th,
                                                                   ALU.mult, ALU.add))
    p.emit("vector", lambda eng: eng.tensor_scalar(tmpk, th, iB, MAGIC, ALU.mult, ALU.add))
    p.emit("vector", lambda eng: eng.tensor_scalar_add(tmpk, tmpk, -MAGIC))
    for C in (CB1, CB2, CB3):
        p.emit("vector", lambda eng, C=C: eng.scalar_tensor_tensor(th, tmpk, -C, th,
                                                                   ALU.mult, ALU.add))
    return p.emit("vector", lambda eng: eng.nop())


def build_L3qkv(nc):
    p = Prog(nc, f32_words=22000, bf16_words=62000)
    i_x = nc.dram_tensor("i_x", [D, TH], F32, kind="ExternalInput").ap()
    w_qkv = nc.dram_tensor("w_qkv", [D, 3072], F32, kind="ExternalInput").ap()
    b_qkv = nc.dram_tensor("b_qkv", [128, 24], F32, kind="ExternalInput").ap()
    i_fr = nc.dram_tensor("i_fr", [128, 4], F32, kind="ExternalInput").ap()
    i_pos = nc.dram_tensor("i_pos", [1, TH], F32, kind="ExternalInput").ap()
    o_q = nc.dram_tensor("o_q", [D, TH], BF16, kind="ExternalOutput").ap()
    o_k = nc.dram_tensor("o_k", [D, TH], BF16, kind="ExternalOutput").ap()
    o_v = nc.dram_tensor("o_v", [D, TH], BF16, kind="ExternalOutput").ap()

    consts = alloc_consts(p)
    ones_a, ones_b, ones1, cst = consts
    bsb = p.f32.alloc(24)
    fr = p.f32.alloc(4)
    posr = p.f32.alloc(TH, parts=1)
    p.dma("sync", bsb, b_qkv)
    p.dma("sync", fr, i_fr)
    p.dma("sync", posr, i_pos)
    prm = ("dS", p.now("dS"))

    cosT = [p.bf16.alloc(TH) for _ in range(4)]
    sinT = [p.bf16.alloc(TH) for _ in range(4)]
    p.f32.mark()
    posrep = p.f32.alloc(TH)
    th = p.f32.alloc(TH)
    tmpk = p.f32.alloc(TH)
    rc = p.f32.alloc(TH)
    mgt = p.f32.alloc(TH)
    pr_done = None
    for tt in range(4):
        n0 = tt * 512
        mm = p.emit("tensor", lambda eng, n0=n0: eng.matmul(p.psum[6][:, 0:512], ones1,
                                                            posr[0:1, n0:n0 + 512],
                                                            start=True, stop=True),
                    waits=[prm, cst] + ([pr_done] if pr_done else []), inc="t")
        pr_done = p.emit("vector", lambda eng, n0=n0: eng.tensor_copy(posrep[:, n0:n0 + 512],
                                                                      p.psum[6][:, 0:512]),
                         waits=[mm])
    for c in range(4):
        tm = p.emit("vector", lambda eng, c=c: eng.tensor_scalar_mul(
            th, posrep, fr[:, c:c + 1]), waits=[pr_done, prm])
        rr = emit_mod2pi(p, th, tmpk, rc, [tm])
        p.emit("vector", lambda eng: eng.tensor_single_scalar(mgt, th, float(np.pi / 2),
                                                              ALU.is_gt))
        p.emit("vector", lambda eng: eng.tensor_scalar_add(rc, th, float(np.pi / 2)))
        p.emit("vector", lambda eng: eng.scalar_tensor_tensor(rc, mgt, float(-2 * np.pi),
                                                              rc, ALU.mult, ALU.add))
        vv = p.emit("vector", lambda eng: eng.nop())
        p.emit("scalar", lambda eng, c=c: eng.activation(sinT[c], th, AF.Sin), waits=[vv])
        p.emit("scalar", lambda eng, c=c: eng.activation(cosT[c], rc, AF.Sin))
        sm = p.emit("scalar", lambda eng: eng.nop())
        p.emit("vector", None, waits=[sm])
    tbl_wait = ("s", p.now("s"))
    tbl_gate = [tbl_wait, ("v", p.now("v"))]
    p.f32.reset()

    xb = [p.bf16.alloc(TH) for _ in range(8)]
    p.f32.mark()
    xrot = [p.f32.alloc(TH) for _ in range(2)]
    cdone = {}
    for c in range(8):
        waits = [cdone[c - 2]] if c - 2 in cdone else list(tbl_gate)
        w = p.dma("sync", xrot[c % 2], i_x[c * 128:(c + 1) * 128, :], waits=waits,
                  sem=f"xl{c % 2}")
        cdone[c] = p.emit("vector", lambda eng, c=c: eng.tensor_copy(xb[c], xrot[c % 2]),
                          waits=[w])
    xb_wait = ("v", p.now("v"))

    qf = [p.bf16.alloc(TH) for _ in range(8)]
    ctx = {}
    rtmp = p.f32.alloc(TH)
    rtmp2 = p.f32.alloc(TH)
    obf = [p.bf16.alloc(TH) for _ in range(2)]
    obf_busy = {0: None, 1: None}
    rope_done = {"m": None}

    def emit_rope_out(ow):
        owr = ow.rearrange("(hh half i) t -> hh half i t", half=2, i=32)
        qk_wait = ("s", p.now("s"))
        last = None
        for c in range(4):
            c2 = c + 4
            p.emit("vector", lambda eng, c=c: eng.tensor_mul(rtmp, qf[c], cosT[c]),
                   waits=[qk_wait, tbl_wait])
            p.emit("vector", lambda eng, c=c: eng.tensor_mul(rtmp2, qf[c], sinT[c]))
            p.emit("vector", lambda eng, c=c, c2=c2: eng.tensor_mul(qf[c], qf[c2], sinT[c]))
            p.emit("vector", lambda eng, c=c: eng.tensor_sub(rtmp, rtmp, qf[c]))
            p.emit("vector", lambda eng, c=c, c2=c2: eng.tensor_mul(qf[c2], qf[c2], cosT[c]))
            p.emit("vector", lambda eng, c=c, c2=c2: eng.tensor_add(qf[c2], qf[c2], rtmp2))
            w1 = [obf_busy[0]] if obf_busy[0] else []
            m1 = p.emit("vector", lambda eng: eng.tensor_copy(obf[0], rtmp), waits=w1)
            obf_busy[0] = p.dma("sync", owr[4 * c:4 * c + 4, 0], obf[0], waits=[m1], sem="ro0")
            w2 = [obf_busy[1]] if obf_busy[1] else []
            m2 = p.emit("vector", lambda eng, c2=c2: eng.tensor_copy(obf[1], qf[c2]), waits=w2)
            obf_busy[1] = p.dma("sync", owr[4 * c:4 * c + 4, 1], obf[1], waits=[m2], sem="ro1")
            last = m2
        rope_done["m"] = last

    def mk_cb():
        def cb(m, tt, psum, n0, n1, wait):
            def fe(eng, m=m, n0=n0, n1=n1, psum=psum):
                return eng.activation(qf[m % 8][:, n0:n1], psum[:, 0:n1 - n0],
                                      AF.Identity, bias=bsb[:, m:m + 1])
            return p.emit("scalar", fe, waits=[wait, prm, rope_done["m"]])
        return cb
    gemm_cols(p, w_qkv, D, xb, xb_wait, list(range(8)), mk_cb(), group_mt=4, ctx=ctx)
    emit_rope_out(o_q)
    gemm_cols(p, w_qkv, D, xb, xb_wait, list(range(8, 16)), mk_cb(), group_mt=4, ctx=ctx)
    emit_rope_out(o_k)
    vdma = {}
    def cb_v(m, tt, psum, n0, n1, wait):
        vb = obf[m % 2]
        def fe(eng, m=m, n0=n0, n1=n1, psum=psum, vb=vb):
            return eng.activation(vb[:, n0:n1], psum[:, 0:n1 - n0], AF.Identity,
                                  bias=bsb[:, m:m + 1])
        mk = p.emit("scalar", fe, waits=[wait, prm, obf_busy[m % 2], rope_done["m"]])
        if n1 == TH:
            obf_busy[m % 2] = p.dma("sync", o_v[(m - 16) * 128:(m - 15) * 128, :], vb,
                                    waits=[mk], sem=f"ro{m % 2}")
        return mk
    gemm_cols(p, w_qkv, D, xb, xb_wait, list(range(16, 24)), cb_v, group_mt=4, ctx=ctx)
    p.drain_dmas()
    p.build()
    return p


def _l3_perm_qkv():
    perm = np.arange(3072)
    for h in range(16):
        for i in range(32):
            perm[h * 32 + i] = h * 64 + i
            perm[512 + h * 32 + i] = h * 64 + 32 + i
            perm[1024 + h * 32 + i] = 1024 + h * 64 + i
            perm[1536 + h * 32 + i] = 1024 + h * 64 + 32 + i
    return perm


def run_L3attn(inp, x6):
    perm = _l3_perm_qkv()
    wqkv = np.ascontiguousarray(np.asarray(inp['L3_attn_w_qkv'], np.float32)[:, perm])
    bp = np.asarray(inp['L3_attn_b_qkv'], np.float32)[perm]
    fr = np.asarray(inp['L3_attn_rope_freqs'], np.float32)[:, :, 0].reshape(512)
    common = {"w_qkv": wqkv, "b_qkv": pack_pp(bp, 24), "i_fr": pack_pp(fr, 4)}
    maps = []
    for core in range(8):
        b, h = core // 2, core % 2
        t0 = h * TH
        m = dict(common)
        m["i_x"] = np.ascontiguousarray(x6[b].T[:, t0:t0 + TH])
        m["i_pos"] = np.arange(t0, t0 + TH, dtype=np.float32).reshape(1, TH)
        maps.append(m)
    r1 = _run(build_L3qkv, maps, "L3qkv")
    Q = {}; Kk = {}; V = {}
    for b in range(B):
        Q[b] = np.concatenate([np.asarray(r1[2 * b]["o_q"]), np.asarray(r1[2 * b + 1]["o_q"])], axis=1)
        Kk[b] = np.concatenate([np.asarray(r1[2 * b]["o_k"]), np.asarray(r1[2 * b + 1]["o_k"])], axis=1)
        V[b] = np.concatenate([np.asarray(r1[2 * b]["o_v"]), np.asarray(r1[2 * b + 1]["o_v"])], axis=1)
    masks = np.zeros((128, 4 * 512), np.float32).astype(BF)
    idb = np.eye(128).astype(BF)
    idf = np.eye(128, dtype=np.float32)
    maps2 = []
    for core in range(8):
        b, g = core // 2, core % 2
        rows = slice(g * 512, (g + 1) * 512)
        maps2.append({"i_q": np.ascontiguousarray(Q[b][rows]),
                      "i_k": np.ascontiguousarray(Kk[b][rows]),
                      "i_v": np.ascontiguousarray(V[b][rows]),
                      "i_masks": masks, "i_idb": idb, "i_idf": idf})
    r2 = _run(lambda nc: build_attn_core(nc, False), maps2, "attn_full")
    maps3 = []
    for core in range(8):
        b, h = core // 2, core % 2
        t0 = h * TH
        ao = np.concatenate([np.asarray(r2[2 * b]["o_out"]),
                             np.asarray(r2[2 * b + 1]["o_out"])], axis=0)
        maps3.append({"i_ao": np.ascontiguousarray(ao[:, t0:t0 + TH]),
                      "i_x": np.ascontiguousarray(x6[b].T[:, t0:t0 + TH]),
                      "w_o": np.ascontiguousarray(inp['L3_attn_w_out'], np.float32),
                      "b_o": pack_pp(inp['L3_attn_b_out'], 8)})
    r3 = _run(build_L2out, maps3, "L2out")
    return _unshard_fm([np.asarray(r["o_x"], np.float32) for r in r3])


def emit_ln_apply_dst(p, chunks, dst_bf, cols, mu, rstd, ready, ones1, cst,
                      lnw, lnb, bank0=0, bank1=1):
    if isinstance(cols[0], int) and len(cols) == 2 and not isinstance(cols[0], list):
        c0s = [cols[0]] * len(chunks)
        TF = cols[1] - cols[0]
    else:
        c0s, TF = cols
    ntt = ceil_div(TF, 512)
    bc_busy = None
    for tt in range(ntt):
        n0, n1 = tt * 512, min(tt * 512 + 512, TF)
        pm, pr = p.psum[bank0], p.psum[bank1]
        def b1(eng, n0=n0, n1=n1, pm=pm):
            return eng.matmul(pm[:, 0:n1 - n0], ones1, mu[0:1, n0:n1],
                              start=True, stop=True)
        p.emit("tensor", b1, waits=[ready, cst] + ([bc_busy] if bc_busy else []))
        def b2(eng, n0=n0, n1=n1, pr=pr):
            return eng.matmul(pr[:, 0:n1 - n0], ones1, rstd[0:1, n0:n1],
                              start=True, stop=True)
        tb = p.emit("tensor", b2, inc="t")
        for c, ch in enumerate(chunks):
            c0 = c0s[c]
            def v1(eng, ch=ch, c=c, c0=c0, n0=n0, n1=n1, pm=pm):
                return eng.tensor_sub(dst_bf[c][:, n0:n1], ch[:, c0 + n0:c0 + n1],
                                      pm[:, 0:n1 - n0])
            p.emit("vector", v1, waits=[tb])
            def v2(eng, c=c, n0=n0, n1=n1, pr=pr):
                return eng.tensor_mul(dst_bf[c][:, n0:n1], dst_bf[c][:, n0:n1],
                                      pr[:, 0:n1 - n0])
            p.emit("vector", v2)
            def v3(eng, c=c, n0=n0, n1=n1):
                return eng.tensor_scalar(dst_bf[c][:, n0:n1], dst_bf[c][:, n0:n1],
                                         lnw[:, c:c + 1], lnb[:, c:c + 1],
                                         ALU.mult, ALU.add)
            m = p.emit("vector", v3)
        bc_busy = m
    return bc_busy


def build_L3ffn_sub(nc):
    TW = 1024
    EX = 513
    TQ = 256 + TW
    TK = 512 + TW
    TX = EX + TW
    TN = TW
    NW = TQ // 256
    NKT = TK // 128
    p = Prog(nc, f32_words=23000, bf16_words=60300)
    i_x = nc.dram_tensor("i_x", [D, TX], F32, kind="ExternalInput").ap()
    w_qkv = nc.dram_tensor("w_qkv", [D, 3072], F32, kind="ExternalInput").ap()
    w_o = nc.dram_tensor("w_o", [D, D], F32, kind="ExternalInput").ap()
    w_1 = nc.dram_tensor("w_1", [D, 5460], F32, kind="ExternalInput").ap()
    w_2 = nc.dram_tensor("w_2", [2730, D], F32, kind="ExternalInput").ap()
    i_qs = nc.dram_tensor("i_qs", [128, 8], F32, kind="ExternalInput").ap()
    i_ks = nc.dram_tensor("i_ks", [128, 8], F32, kind="ExternalInput").ap()
    i_bs2 = nc.dram_tensor("i_bs2", [128, 8], F32, kind="ExternalInput").ap()
    i_csq = nc.dram_tensor("i_csq", [128, TQ], F32, kind="ExternalInput").ap()
    i_snq = nc.dram_tensor("i_snq", [128, TQ], F32, kind="ExternalInput").ap()
    i_csk = nc.dram_tensor("i_csk", [128, TK], F32, kind="ExternalInput").ap()
    i_snk = nc.dram_tensor("i_snk", [128, TK], F32, kind="ExternalInput").ap()
    i_rot = nc.dram_tensor("i_rot", [128, 128], F32, kind="ExternalInput").ap()
    i_idb = nc.dram_tensor("i_idb", [128, 128], F32, kind="ExternalInput").ap()
    i_idf = nc.dram_tensor("i_idf", [128, 128], F32, kind="ExternalInput").ap()
    i_sel2 = nc.dram_tensor("i_sel2", [128, 2], F32, kind="ExternalInput").ap()
    i_sel2t = nc.dram_tensor("i_sel2t", [2, 128], F32, kind="ExternalInput").ap()
    i_masks = nc.dram_tensor("i_masks", [128, NW * 4 * 256], F32, kind="ExternalInput").ap()
    i_zrow = nc.dram_tensor("i_zrow", [1, TQ], F32, kind="ExternalInput").ap()
    i_lng = nc.dram_tensor("i_lng", [128, 8], F32, kind="ExternalInput").ap()
    i_lnb = nc.dram_tensor("i_lnb", [128, 8], F32, kind="ExternalInput").ap()
    i_b1 = nc.dram_tensor("i_b1", [128, 44], F32, kind="ExternalInput").ap()
    i_b2 = nc.dram_tensor("i_b2", [128, 8], F32, kind="ExternalInput").ap()
    o_x = nc.dram_tensor("o_x", [D, TW], F32, kind="ExternalOutput").ap()
    scr_v = nc.dram_tensor("scr_v", [D, TK], BF16, kind="Internal").ap()
    scr_ao = nc.dram_tensor("scr_ao", [D, TQ], BF16, kind="Internal").ap()

    consts = alloc_consts(p)
    ones_a, ones_b, ones1, cst = consts
    onecolb = p.bf16.alloc(1)
    p.emit("vector", lambda eng: eng.memset(onecolb, 1.0))
    cst2 = p.emit("vector", lambda eng: eng.nop())
    qs_sb = p.f32.alloc(8)
    ks_sb = p.f32.alloc(8)
    bs2_sb = p.f32.alloc(8)
    lng_sb = p.f32.alloc(8)
    lnb_sb = p.f32.alloc(8)
    b1_sb = p.f32.alloc(44)
    b2_sb = p.f32.alloc(8)
    idf = p.f32.alloc(128)
    sel2 = p.f32.alloc(2)
    sel2t = p.f32.alloc(128, parts=2)
    zrow = p.f32.alloc(TQ, parts=1)
    for dst, src_ in ((qs_sb, i_qs), (ks_sb, i_ks), (bs2_sb, i_bs2), (lng_sb, i_lng),
                      (lnb_sb, i_lnb), (b1_sb, i_b1), (b2_sb, i_b2), (idf, i_idf),
                      (sel2, i_sel2), (sel2t, i_sel2t), (zrow, i_zrow)):
        p.dma("sync", dst, src_)
    prm = ("dS", p.now("dS"))
    rotm = p.bf16.alloc(128)
    idb = p.bf16.alloc(128)
    masks = p.bf16.alloc(NW * 4 * 256)
    psb = p.bf16.alloc(1024)
    obuf = p.bf16.alloc(256, parts=64)
    vt = p.bf16.alloc(NKT * 64)
    ksc = p.bf16.alloc(128)
    for dst, src_ in ((rotm, i_rot), (idb, i_idb), (masks, i_masks)):
        p.dma("gpsimd", dst, src_, sem="tb2")
    tbw = ("tb2", p.now("tb2"))

    x = [p.f32.alloc(TX) for _ in range(8)]
    xw = None
    for c in range(8):
        xw = p.dma("sync", x[c], i_x[c * 128:(c + 1) * 128, :], sem="xl")
    ctx2 = {"gbuf": [[p.bf16.alloc(4 * 128) for _ in range(8)] for _ in range(2)],
            "gsem": [f"gc{next(_WSEM)}_0", f"gc{next(_WSEM)}_1"],
            "gfree": [None, None], "psum_busy": {}}

    p.bf16.mark()
    qf = [p.bf16.alloc(TQ) for _ in range(8)]
    kf = [p.bf16.alloc(TK) for _ in range(8)]
    pv = {}

    p.bf16.mark()
    xs = [p.bf16.alloc(TK) for _ in range(8)]
    for c in range(8):
        off = 1 if c < 4 else 0
        p.emit("vector", lambda eng, c=c, off=off:
               eng.tensor_copy(xs[c], x[c][:, off:off + TK]), waits=[xw])
    xs_wait = ("v", p.now("v"))
    ctx = ctx2
    vrot = [p.bf16.alloc(TK) for _ in range(2)]
    def cb_q(m, tt, psum, n0, n1, wait):
        return p.emit("vector", lambda eng, m=m, n0=n0, n1=n1, psum=psum:
                      eng.tensor_copy(qf[m][:, n0:n1], psum[:, 0:n1 - n0]), waits=[wait])
    def cb_k(m, tt, psum, n0, n1, wait):
        return p.emit("vector", lambda eng, m=m, n0=n0, n1=n1, psum=psum:
                      eng.tensor_copy(kf[m - 8][:, n0:n1], psum[:, 0:n1 - n0]), waits=[wait])
    def cb_v(m, tt, psum, n0, n1, wait):
        vb = vrot[m % 2]
        mk = p.emit("vector", lambda eng, n0=n0, n1=n1, psum=psum, vb=vb:
                    eng.tensor_copy(vb[:, n0:n1], psum[:, 0:n1 - n0]),
                    waits=[wait, pv.get(("vd", m % 2))])
        if n1 == TK:
            pv[("vd", m % 2)] = p.dma("sync", scr_v[(m - 16) * 128:(m - 15) * 128, :], vb,
                                      waits=[mk], sem=f"vs{m % 2}")
        return mk
    rhs_q = [xs[c][:, 256:256 + TQ] for c in range(8)]
    gemm_cols(p, w_qkv, D, rhs_q, xs_wait, list(range(8)), cb_q, group_mt=4, ctx=ctx)
    gemm_cols(p, w_qkv, D, xs, xs_wait, list(range(8, 16)), cb_k, group_mt=4, ctx=ctx)
    gemm_cols(p, w_qkv, D, xs, xs_wait, list(range(16, 24)), cb_v, group_mt=4, ctx=ctx)
    qkv_wait = ("v", p.now("v"))
    p.barrier()
    p.bf16.reset()

    p.bf16.mark()
    csq = p.bf16.alloc(TQ)
    snq = p.bf16.alloc(TQ)
    csk = p.bf16.alloc(TK)
    snk = p.bf16.alloc(TK)
    for dst, src_ in ((csq, i_csq), (snq, i_snq), (csk, i_csk), (snk, i_snk)):
        p.dma("gpsimd", dst, src_, sem="tb3")
    tbw3 = ("tb3", p.now("tb3"))
    nrm = p.f32.alloc(TK, parts=2)
    sqb = p.f32.alloc(TK)
    for fchunks, NQ, cs, sn, sc_pp in ((qf, TQ, csq, snq, qs_sb), (kf, TK, csk, snk, ks_sb)):
        for c in range(8):
            ch = fchunks[c]
            s1 = p.emit("scalar", lambda eng, ch=ch, NQ=NQ: eng.activation(
                sqb[:, 0:NQ], ch, AF.Square),
                waits=[qkv_wait, pv.get("nm"), pv.get("ropv")])
            mm = None
            for tt in range(ceil_div(NQ, 512)):
                n0, n1 = tt * 512, min(tt * 512 + 512, NQ)
                def fnm(eng, n0=n0, n1=n1):
                    return eng.matmul(p.psum[6][0:2, 0:n1 - n0], sel2,
                                      sqb[:, n0:n1], start=True, stop=True)
                mm = p.emit("tensor", fnm, waits=[s1, prm, pv.get("nev")], inc="t")
                pv["nev"] = p.emit("vector", lambda eng, n0=n0, n1=n1:
                                   eng.tensor_copy(nrm[:, n0:n1], p.psum[6][0:2, 0:n1 - n0]),
                                   waits=[mm])
            pv["nm"] = mm
            sq2 = p.emit("scalar", lambda eng, NQ=NQ: eng.activation(nrm[:, 0:NQ],
                                                                     nrm[:, 0:NQ], AF.Sqrt),
                         waits=[pv["nev"]])
            v1 = p.emit("vector", lambda eng, NQ=NQ: eng.tensor_single_scalar(
                nrm[:, 0:NQ], nrm[:, 0:NQ], 1e-12, ALU.max), waits=[sq2])
            v2 = p.emit("vector", lambda eng, NQ=NQ: eng.reciprocal(nrm[0:2, 0:NQ],
                                                                    nrm[0:2, 0:NQ]))
            for tt in range(ceil_div(NQ, 512)):
                n0, n1 = tt * 512, min(tt * 512 + 512, NQ)
                mrr = p.emit("tensor", lambda eng, n0=n0, n1=n1: eng.matmul(
                    p.psum[5][:, 0:n1 - n0], sel2t, nrm[:, n0:n1], start=True, stop=True),
                    waits=[v2, prm, pv.get("hatv")], inc="t")
                hv = p.emit("vector", lambda eng, ch=ch, c=c, n0=n0, n1=n1, sc_pp=sc_pp:
                            eng.scalar_tensor_tensor(ch[:, n0:n1], ch[:, n0:n1],
                                                     sc_pp[:, c:c + 1],
                                                     p.psum[5][:, 0:n1 - n0],
                                                     ALU.mult, ALU.mult), waits=[mrr])
                pv["hatv"] = hv
            for tt in range(ceil_div(NQ, 512)):
                n0, n1 = tt * 512, min(tt * 512 + 512, NQ)
                mro = p.emit("tensor", lambda eng, ch=ch, n0=n0, n1=n1: eng.matmul(
                    p.psum[4][:, 0:n1 - n0], rotm, ch[:, n0:n1], start=True, stop=True),
                    waits=[pv["hatv"], tbw, pv.get("rotb")], inc="t")
                r1 = p.emit("vector", lambda eng, n0=n0, n1=n1, sn=sn: eng.tensor_tensor(
                    sqb[:, n0:n1], p.psum[4][:, 0:n1 - n0], sn[:, n0:n1], ALU.mult),
                    waits=[mro, tbw3])
                pv["rotb"] = r1
                r2 = p.emit("vector", lambda eng, ch=ch, n0=n0, n1=n1, cs=cs:
                            eng.tensor_mul(ch[:, n0:n1], ch[:, n0:n1], cs[:, n0:n1]))
                rv = p.emit("vector", lambda eng, ch=ch, n0=n0, n1=n1: eng.tensor_add(
                    ch[:, n0:n1], ch[:, n0:n1], sqb[:, n0:n1]))
                pv["ropv"] = rv
    rope_wait = ("v", p.now("v"))
    p.barrier()
    p.bf16.reset()

    p.bf16.mark()
    vtmp = [p.bf16.alloc(TK, parts=64) for _ in range(2)]
    aoch = [p.bf16.alloc(TQ) for _ in range(2)]
    vcast = p.f32.alloc(128, parts=64)
    lrec = p.f32.alloc(256, parts=1)
    for h in range(16):
        rb = (h % 2) * 64
        qu = qf[h // 2][rb:rb + 64, :]
        ku = kf[h // 2][rb:rb + 64, :]
        vw = p.dma("gpsimd", vtmp[h % 2], scr_v[h * 64:(h + 1) * 64, :],
                   waits=[("vs0", p.now("vs0")), ("vs1", p.now("vs1"))] if h < 2
                   else [pv.get(("vtm", h % 2))], sem=f"vr{h % 2}")
        for kt in range(NKT):
            cv = p.emit("vector", lambda eng, kt=kt, h=h:
                        eng.tensor_copy(vcast, vtmp[h % 2][:, kt * 128:(kt + 1) * 128]),
                        waits=[vw, pv.get("vtm2")])
            tv = p.emit("tensor", lambda eng: eng.transpose(p.psum[6][0:128, 0:64], vcast,
                                                            idf[0:64, 0:64]),
                        waits=[cv, prm, pv.get("vte")], inc="t")
            pv["vtm2"] = tv
            pv["vte"] = p.emit("vector", lambda eng, kt=kt:
                               eng.tensor_copy(vt[:, kt * 64:(kt + 1) * 64],
                                               p.psum[6][0:128, 0:64]),
                               waits=[tv, pv.get("omprev")])
        pv[("vtm", h % 2)] = pv["vte"]
        vt_wait = pv["vte"]
        for w in range(NW):
            q0 = w * 256
            m2 = None
            for kt in range(4):
                kc0 = q0 + kt * 128
                stp = p.psumT.ap()[:, kt * 256:(kt + 1) * 256]
                w0 = [rope_wait] + ([pv["expm"]] if (kt == 0 and "expm" in pv) else [])
                if kt < 2:
                    kv = p.emit("vector", lambda eng, kc0=kc0, ku=ku, rb=rb:
                                eng.tensor_scalar_mul(ksc[rb:rb + 64, :],
                                                      ku[:, kc0:kc0 + 128],
                                                      bs2_sb[0:64, 0:1]),
                                waits=[rope_wait, prm, pv.get("kscm")])
                    m1 = p.emit("tensor", lambda eng, q0=q0, qu=qu, stp=stp, rb=rb:
                                eng.matmul(stp[:, 0:256], ksc[rb:rb + 64, :],
                                           qu[:, q0:q0 + 256],
                                           start=True, stop=False),
                                waits=w0 + [kv], inc="t")
                    pv["kscm"] = m1
                else:
                    m1 = p.emit("tensor", lambda eng, kc0=kc0, q0=q0, qu=qu, ku=ku, stp=stp:
                                eng.matmul(stp[:, 0:256], ku[:, kc0:kc0 + 128],
                                           qu[:, q0:q0 + 256], start=True, stop=False),
                                waits=w0)
                def fmask(eng, w=w, kt=kt, stp=stp):
                    mi = (w * 4 + kt) * 256
                    return eng.matmul(stp[:, 0:256], idb, masks[:, mi:mi + 256],
                                      start=False, stop=True)
                m2 = p.emit("tensor", fmask, waits=[tbw], inc="t" if kt == 3 else None)
            ex = p.emit("scalar", lambda eng: eng.activation(psb, p.psumT.ap()[:, 0:1024],
                                                             AF.Exp, scale=8.0),
                        waits=[m2, pv.get("pvv")])
            pv["expm"] = ex
            lm = om = None
            for kt in range(4):
                lm = p.emit("tensor", lambda eng, kt=kt: eng.matmul(
                    p.psum[4][0:1, 0:256], onecolb, psb[:, kt * 256:(kt + 1) * 256],
                    start=(kt == 0), stop=(kt == 3)),
                    waits=[ex, cst2] + ([pv["lev"]] if (kt == 0 and "lev" in pv) else []),
                    inc="t" if kt == 3 else None)
                om = p.emit("tensor", lambda eng, kt=kt, w=w: eng.matmul(
                    p.psum[5][0:64, 0:256], vt[:, (2 * w + kt) * 64:(2 * w + kt + 1) * 64],
                    psb[:, kt * 256:(kt + 1) * 256], start=(kt == 0), stop=(kt == 3)),
                    waits=[vt_wait] + ([pv["oev"]] if (kt == 0 and "oev" in pv) else []),
                    inc="t" if kt == 3 else None)
            le = p.emit("vector", lambda eng: eng.tensor_copy(lrec, p.psum[4][0:1, 0:256]),
                        waits=[lm])
            pv["lev"] = le
            rc = p.emit("vector", lambda eng: eng.reciprocal(lrec, lrec))
            mr = p.emit("tensor", lambda eng: eng.matmul(p.psum[7][0:64, 0:256],
                                                         ones1[0:1, 0:64], lrec,
                                                         start=True, stop=True),
                        waits=[rc, cst, pv.get("rrev")], inc="t")
            oe = p.emit("vector", lambda eng: eng.tensor_copy(obuf, p.psum[5][0:64, 0:256]),
                        waits=[om])
            pv["oev"] = oe
            wz = [pv[("aod", h % 2)]] if (w == 0 and h >= 2 and (("aod", h % 2) in pv)) else []
            pv["rrev"] = p.emit("vector", lambda eng, h=h, w=w: eng.tensor_tensor(
                aoch[h % 2][0:64, w * 256:(w + 1) * 256],
                obuf, p.psum[7][0:64, 0:256], ALU.mult), waits=[mr] + wz)
            pv["pvv"] = pv["rrev"]
        pv["omprev"] = om
        pv[("aod", h % 2)] = p.dma("sync", scr_ao[h * 64:(h + 1) * 64, :],
                                   aoch[h % 2][0:64, :], waits=[pv["rrev"]],
                                   sem=f"ad{h % 2}")
    p.barrier()
    p.bf16.reset()
    p.bf16.reset()

    p.bf16.mark()
    ao = [p.bf16.alloc(TQ) for _ in range(8)]
    aw = None
    for c in range(8):
        aw = p.dma("gpsimd", ao[c], scr_ao[c * 128:(c + 1) * 128, :],
                   waits=[("ad0", p.now("ad0")), ("ad1", p.now("ad1"))] if c == 0 else [],
                   sem="aor")
    zrep = p.bf16.alloc(TQ)
    wtmp = p.f32.alloc(512)
    for tt in range(ceil_div(TQ, 512)):
        n0, n1 = tt * 512, min(tt * 512 + 512, TQ)
        mzz = p.emit("tensor", lambda eng, n0=n0, n1=n1: eng.matmul(
            p.psum[6][:, 0:n1 - n0], ones1, zrow[0:1, n0:n1], start=True, stop=True),
            waits=[prm, cst, pv.get("zev")], inc="t")
        pv["zev"] = p.emit("vector", lambda eng, n0=n0, n1=n1: eng.tensor_copy(
            zrep[:, n0:n1], p.psum[6][:, 0:n1 - n0]), waits=[mzz])
    zw = pv["zev"]
    def cb_o(m, tt, psum, n0, n1, wait):
        e1 = p.emit("vector", lambda eng, n0=n0, n1=n1, psum=psum:
                    eng.tensor_tensor(wtmp[:, 0:n1 - n0], psum[:, 0:n1 - n0],
                                      zrep[:, n0:n1], ALU.mult), waits=[wait, zw])
        return p.emit("vector", lambda eng, m=m, n0=n0, n1=n1:
                      eng.tensor_add(x[m][:, 257 + n0:257 + n1],
                                     x[m][:, 257 + n0:257 + n1], wtmp[:, 0:n1 - n0]))
    gemm_cols(p, w_o, D, ao, aw, list(range(8)), cb_o, group_mt=4, ctx=ctx2)
    xp_wait = ("v", p.now("v"))
    p.barrier()
    p.bf16.reset()

    xn = [p.bf16.alloc(TN) for _ in range(8)]
    offs = [513 if c < 4 else 512 for c in range(8)]
    xch = [x[c][:, offs[c]:offs[c] + TN] for c in range(8)]
    mu, rstd, ready = ln_stats_resident(p, xch, TN, ones_a, ones_b, cst,
                                        ready_waits=[xp_wait], banks=(3, 4, 5, 6, 7))
    emit_ln_apply_dst(p, x, xn, (offs, TN), mu, rstd, ready, ones1, cst,
                      lng_sb, lnb_sb)
    xn_wait = ("v", p.now("v"))
    p.barrier()
    for half in range(2):
        a0 = half * 1365
        acols = [(a0 + j * 128, min(a0 + (j + 1) * 128, a0 + 1365)) for j in range(11)]
        gcols = [(2730 + c0, 2730 + c1) for (c0, c1) in acols]
        p.bf16.mark()
        h2 = [p.bf16.alloc(TN) for _ in range(11)]
        gtmp = p.bf16.alloc(512)
        def cb_a(m, tt, psum, n0, n1, wait, half=half, acols=acols):
            rows = acols[m][1] - acols[m][0]
            def fe(eng, m=m, n0=n0, n1=n1, psum=psum, rows=rows):
                return eng.activation(h2[m][0:rows, n0:n1], psum[0:rows, 0:n1 - n0],
                                      AF.Identity,
                                      bias=b1_sb[0:rows, half * 22 + m:half * 22 + m + 1])
            return p.emit("scalar", fe, waits=[wait, prm, pv.get("w2done")])
        gemm_cols(p, w_1, D, xn, xn_wait, list(range(11)), cb_a, group_mt=4, ctx=ctx2,
                  mcols=acols)
        def cb_g(m, tt, psum, n0, n1, wait, half=half, acols=acols):
            rows = acols[m][1] - acols[m][0]
            def fe(eng, m=m, n0=n0, n1=n1, psum=psum, rows=rows):
                return eng.activation(gtmp[0:rows, 0:n1 - n0], psum[0:rows, 0:n1 - n0],
                                      AF.Gelu,
                                      bias=b1_sb[0:rows, half * 22 + 11 + m:half * 22 + 12 + m])
            s1 = p.emit("scalar", fe, waits=[wait, prm, pv.get("gt")])
            def fmu(eng, m=m, n0=n0, n1=n1, rows=rows):
                return eng.tensor_mul(h2[m][0:rows, n0:n1], h2[m][0:rows, n0:n1],
                                      gtmp[0:rows, 0:n1 - n0])
            mk = p.emit("vector", fmu, waits=[s1])
            pv["gt"] = mk
            return mk
        gemm_cols(p, w_1, D, xn, xn_wait, list(range(11)), cb_g, group_mt=4, ctx=ctx2,
                  mcols=gcols)
        h2_wait = ("v", p.now("v"))
        w2s, w2w = load_weight_slabs(p, w_2[a0:a0 + 1365, :], 1365, D,
                                     gate=pv.get("w2done"))
        h2r = [h2[k][0:(85 if k == 10 else 128), :] for k in range(11)]
        def cb_w2(m, tt, psum, n0, n1, wait):
            return p.emit("vector", lambda eng, m=m, n0=n0, n1=n1, psum=psum:
                          eng.tensor_add(x[m][:, 513 + n0:513 + n1],
                                         x[m][:, 513 + n0:513 + n1], psum), waits=[wait])
        gemm_fm(p, w2s, w2w, h2r, 8, cb_w2)
        pv["w2done"] = ("v", p.now("v"))
        p.barrier()
        p.bf16.reset()
    for c in range(8):
        vb = p.emit("vector", lambda eng, c=c: eng.tensor_scalar_add(
            x[c][:, 513:513 + TW], x[c][:, 513:513 + TW], b2_sb[:, c:c + 1]),
            waits=[prm])
        p.dma("sync", o_x[c * 128:(c + 1) * 128, :], x[c][:, 513:513 + TW], waits=[vb],
              sem="ox")
    p.drain_dmas()
    p.build()
    return p


def build_addln(nc):
    p = Prog(nc, f32_words=36000, bf16_words=8000)
    i_a = nc.dram_tensor("i_a", [D, TH], F32, kind="ExternalInput").ap()
    i_x = nc.dram_tensor("i_x", [D, TH], F32, kind="ExternalInput").ap()
    o_x = nc.dram_tensor("o_x", [D, TH], F32, kind="ExternalOutput").ap()
    consts = alloc_consts(p)
    acc = [p.f32.alloc(TH) for _ in range(8)]
    aw = None
    for c in range(8):
        aw = p.dma("sync", acc[c], i_a[c * 128:(c + 1) * 128, :], sem="al")
    emit_post_ln_stream(p, acc, i_x, o_x, consts, aw)
    p.drain_dmas()
    p.build()
    return p


def _l3ffn_consts():
    j32 = np.arange(32, dtype=np.float64)
    bs = (2 * j32 + 0.4 * 64) / (1.4 * 64)
    inv_freq = 1.0 / 10000.0 ** (2 * j32 / 64.0)
    rot = np.zeros((128, 128), np.float32)
    for blk in range(2):
        o = blk * 64
        for i in range(32):
            rot[o + 32 + i, o + i] = -1.0
            rot[o + i, o + 32 + i] = 1.0
    sel2 = np.zeros((128, 2), np.float32)
    sel2[0:64, 0] = 1.0
    sel2[64:128, 1] = 1.0
    return bs, inv_freq, rot, sel2


def _l3ffn_tables(t0, TQ, TK):
    bs, inv_freq, rot, sel2 = _l3ffn_consts()
    rowi = np.arange(128) % 32
    def tab(g0, N, inv):
        tg = np.arange(g0, g0 + N, dtype=np.float64)
        ang = np.float32(tg[None, :] * inv_freq[rowi][:, None]).astype(np.float64)
        prec = (tg % 256) / 128.0
        sc = bs[rowi][:, None] ** (-prec[None, :] if inv else prec[None, :])
        return (np.cos(ang) * sc).astype(np.float32), (np.sin(ang) * sc).astype(np.float32)
    csq, snq = tab(t0 - 256, TQ, False)
    csk, snk = tab(t0 - 512, TK, True)
    return csq, snq, csk, snk


def _l3ffn_masks(t0, NW):
    jj = np.arange(128)
    ii = np.arange(256)
    out = np.zeros((128, NW * 4 * 256), np.float32)
    W0 = t0 // 256
    for w in range(NW):
        special = (W0 + w - 2) < 0
        for kt in range(4):
            jg = kt * 128 + jj
            ok = (ii[None, :] <= jg[:, None]) & (jg[:, None] <= ii[None, :] + 256)
            if special:
                ok &= (jg[:, None] >= 256)
            out[:, (w * 4 + kt) * 256:(w * 4 + kt + 1) * 256] = np.where(ok, 0.0, -1e30)
    return out


def run_L3ffn(inp, x7):
    TW, EX, TQ, TK = 1024, 513, 1280, 1536
    bs, inv_freq, rot, sel2 = _l3ffn_consts()
    rowd = np.arange(128) % 64
    bs2 = np.zeros((128, 8), np.float32)
    bs2[:, 0] = (bs[np.arange(128) % 32] ** 2).astype(np.float32)
    x = x7.copy()
    for i in range(6):
        b1v = np.asarray(inp['L3_ffn_b1'], np.float32)[i]
        b1p = np.zeros((128, 44), np.float32)
        for half in range(2):
            a0 = half * 1365
            for j in range(11):
                c0, c1 = a0 + j * 128, min(a0 + (j + 1) * 128, a0 + 1365)
                b1p[0:c1 - c0, half * 22 + j] = b1v[c0:c1]
                b1p[0:c1 - c0, half * 22 + 11 + j] = b1v[2730 + c0:2730 + c1]
        common = {
            "w_qkv": np.ascontiguousarray(inp['L3_ffn_Wqkv'][i], dtype=np.float32),
            "w_o": np.ascontiguousarray(inp['L3_ffn_Wo'][i], dtype=np.float32),
            "w_1": np.ascontiguousarray(inp['L3_ffn_W1'][i], dtype=np.float32),
            "w_2": np.ascontiguousarray(inp['L3_ffn_W2'][i], dtype=np.float32),
            "i_qs": np.tile(np.asarray(inp['L3_ffn_q_scale'], np.float32)[i][rowd % 64].reshape(128, 1), (1, 8)),
            "i_ks": np.tile(np.asarray(inp['L3_ffn_k_scale'], np.float32)[i][rowd % 64].reshape(128, 1), (1, 8)),
            "i_bs2": bs2,
            "i_rot": rot, "i_idb": np.eye(128, dtype=np.float32),
            "i_idf": np.eye(128, dtype=np.float32),
            "i_sel2": sel2, "i_sel2t": np.ascontiguousarray(sel2.T),
            "i_lng": pack_pp(inp['L3_ffn_ln_g'][i], 8),
            "i_lnb": pack_pp(inp['L3_ffn_ln_b'][i], 8),
            "i_b1": b1p, "i_b2": pack_pp(inp['L3_ffn_b2'][i], 8),
        }
        xn = np.empty_like(x)
        for q in range(2):
            maps = []
            for core in range(8):
                b, h = core // 2, core % 2
                t0 = h * TH + q * TW
                m = dict(common)
                xt = x[b].T
                if t0 == 0:
                    xh = np.concatenate([np.zeros((D, EX), np.float32), xt[:, 0:TW]], axis=1)
                else:
                    xh = xt[:, t0 - EX:t0 + TW]
                m["i_x"] = np.ascontiguousarray(xh)
                csq, snq, csk, snk = _l3ffn_tables(t0, TQ, TK)
                m["i_csq"], m["i_snq"], m["i_csk"], m["i_snk"] = csq, snq, csk, snk
                m["i_masks"] = _l3ffn_masks(t0, 5)
                zr = np.ones((1, TQ), np.float32)
                if t0 == 0:
                    zr[0, 0:256] = 0.0
                m["i_zrow"] = zr
                maps.append(m)
            res = _run(build_L3ffn_sub, maps, "L3ffn")
            for core in range(8):
                b, h = core // 2, core % 2
                t0 = h * TH + q * TW
                xn[b, t0:t0 + TW] = np.asarray(res[core]["o_x"], np.float32).T
        x = xn
    maps = []
    for core in range(8):
        b, h = core // 2, core % 2
        t0 = h * TH
        maps.append({"i_a": np.ascontiguousarray(x[b].T[:, t0:t0 + TH]),
                     "i_x": np.ascontiguousarray(x7[b].T[:, t0:t0 + TH])})
    res = _run(build_addln, maps, "addln")
    return _unshard_fm([np.asarray(r["o_x"], np.float32) for r in res])


def kernel(**inputs):
    inp = {k: np.asarray(v) for k, v in inputs.items()}
    x = np.asarray(inp["L0_attn_x"], np.float32)
    x1 = run_L0(inp, x)
    x2, rw = run_L0ffn(inp, x1)
    x3 = run_L1attn(inp, x2, rw)
    x4 = run_L1ffn(inp, x3)
    x5 = run_L2attn(inp, x4)
    x6 = run_L2ffn(inp, x5)
    x7 = run_L3attn(inp, x6)
    x8 = run_L3ffn(inp, x7)
    return x8.astype(np.float32)
